# version 13
# baseline (speedup 1.0000x reference)
import contextlib
import numpy as np
import ml_dtypes
import concourse.bass as bass
import concourse.mybir as mybir
from concourse.bass import IndirectOffsetOnAxis
from concourse.bass_utils import run_bass_kernel_spmd

F32 = mybir.dt.float32
BF16 = mybir.dt.bfloat16
I32 = mybir.dt.int32
AF = mybir.ActivationFunctionType
ALU = mybir.AluOpType
AX = mybir.AxisListType.X

B, S, D = 2, 4096, 4096
QH, KVH, TS = 8, 2, 1024
HFF, DQ, D2 = 11008, 1024, 2752
NHT = 86
SCALE = 1.0 / float(np.sqrt(128.0))
RG = [[0, 1, 2, 3], [4, 5, 6, 7]]
ENGS = ("sync", "tensor", "vector", "scalar", "gpsimd")


class Rec:
    def __init__(self):
        self.q = {e: [] for e in ENGS}

    def on(self, eng, fn):
        self.q[eng].append(fn)

    def wait(self, eng, sem, val):
        if val > 0:
            self.q[eng].append(lambda b, s=sem.h, v=val: b.wait_ge(s, v))

    def flush(self, blk):
        for e in ENGS:
            fns = self.q[e]
            if fns:
                getattr(blk, e)(lambda bld, fns=tuple(fns): [f(bld) for f in fns])
        self.q = {e: [] for e in ENGS}


class Sem:
    def __init__(self, nc, es, name):
        self.h = es.enter_context(nc.semaphore(name))
        self.v = 0


def _i(fn, sem, k=1):
    sem.v += k
    return lambda b, fn=fn, s=sem.h, k=k: fn(b).then_inc(s, k)


def fold_segments(h):
    segs = []
    i = 0
    while i < 128:
        g = 128 * h + i
        dst = g % D2
        L = min(128 - i, D2 - dst, 128 - (dst % 128))
        segs.append((i, L, dst))
        i += L
    return segs


def build_nc(debug=False):
    nc = bass.Bass("TRN2", target_bir_lowering=False)

    xt = nc.dram_tensor("xt", [D, S], F32, kind="ExternalInput").ap()
    xts = nc.dram_tensor("xts", [D, TS], F32, kind="ExternalInput").ap()
    wqT = nc.dram_tensor("wqT", [D, 1024], F32, kind="ExternalInput").ap()
    wkT = nc.dram_tensor("wkT", [D, 256], F32, kind="ExternalInput").ap()
    wvT = nc.dram_tensor("wvT", [D, 256], F32, kind="ExternalInput").ap()
    woT = nc.dram_tensor("woT", [D, D], F32, kind="ExternalInput").ap()
    w1qT = nc.dram_tensor("w1qT", [DQ, HFF], F32, kind="ExternalInput").ap()
    w3qT = nc.dram_tensor("w3qT", [DQ, HFF], F32, kind="ExternalInput").ap()
    w2qT = nc.dram_tensor("w2qT", [D2, D], F32, kind="ExternalInput").ap()
    a1R = nc.dram_tensor("a1R", [128, 64], F32, kind="ExternalInput").ap()
    a3R = nc.dram_tensor("a3R", [128, 64], F32, kind="ExternalInput").ap()
    a2R = nc.dram_tensor("a2R", [128, 172], F32, kind="ExternalInput").ap()
    b1T = nc.dram_tensor("b1T", [2, HFF], F32, kind="ExternalInput").ap()
    b3T = nc.dram_tensor("b3T", [2, HFF], F32, kind="ExternalInput").ap()
    b2T = nc.dram_tensor("b2T", [2, D], F32, kind="ExternalInput").ap()
    cosT = nc.dram_tensor("cosT", [128, S], F32, kind="ExternalInput").ap()
    sinsT = nc.dram_tensor("sinsT", [128, S], F32, kind="ExternalInput").ap()
    m0h = nc.dram_tensor("m0", [128, 128], F32, kind="ExternalInput").ap()
    idh = nc.dram_tensor("ident", [128, 128], F32, kind="ExternalInput").ap()
    idxg = nc.dram_tensor("idxg", [128, 32], I32, kind="ExternalInput").ap()
    outT = nc.dram_tensor("outT", [D, TS], F32, kind="ExternalOutput").ap()

    rkind = "ExternalOutput" if debug else "Internal"
    res_hbm = nc.dram_tensor("res_hbm", [D, TS], F32, kind=rkind).ap()
    res2_hbm = nc.dram_tensor("res2_hbm", [D, TS], F32, kind=rkind).ap()
    qt_hbm = nc.dram_tensor("qt_hbm", [1024, S], BF16).ap()
    ag_in = nc.dram_tensor("ag_in", [1024, S], BF16)
    ag_out_g = [nc.dram_tensor("ag_out0", [2048, S], BF16), nc.dram_tensor("ag_out1", [1536, S], BF16), nc.dram_tensor("ag_out2", [512, S], BF16)]
    if debug:
        dbgq = nc.dram_tensor("dq", [128, QH * S], F32, kind="ExternalOutput").ap()
        dbgk = nc.dram_tensor("dk", [128, KVH * S], F32, kind="ExternalOutput").ap()
        dbgv = nc.dram_tensor("dv", [128, KVH * 32 * 129], F32, kind="ExternalOutput").ap()
        dbgao = nc.dram_tensor("dao", [1024, S], F32, kind="ExternalOutput").ap()
        dbggo = nc.dram_tensor("dgo", [128, 32 * 1024], F32, kind="ExternalOutput").ap()
        dbgst = nc.dram_tensor("dst_o", [1, 6144], F32, kind="ExternalOutput").ap()
        dbgxf = nc.dram_tensor("dxf", [128, 8 * 1024], F32, kind="ExternalOutput").ap()
        dbggf = nc.dram_tensor("dgf", [128, 22 * 1024], F32, kind="ExternalOutput").ap()
        dbgyl = nc.dram_tensor("dyl", [2, 2048], F32, kind="ExternalOutput").ap()
        dbgbias = nc.dram_tensor("dbias", [128, 8], F32, kind="ExternalOutput").ap()
        dbgcoll = nc.dram_tensor("dcoll", [1, 80], F32, kind="ExternalOutput").ap()
        dbgS = nc.dram_tensor("dS", [128, 4096], F32, kind="ExternalOutput").ap()
        dbgAO = nc.dram_tensor("dAO", [128, 2560], F32, kind="ExternalOutput").ap()

    ES = contextlib.ExitStack()
    with ES:
        sb = lambda n, sh, dt, **kw: ES.enter_context(nc.sbuf_tensor(n, sh, dt, **kw))
        ident_bf = sb("ident_bf", [128, 128], BF16)
        m0_sb = sb("m0_sb", [128, 128], F32)
        ones_c = sb("ones_c", [128, 1], BF16)
        onesr_f = sb("onesr_f", [1, 128], F32)
        onesr_b = sb("onesr_b", [1, 128], BF16)
        a1_sb = sb("a1_sb", [128, 64], BF16)
        a3_sb = sb("a3_sb", [128, 64], BF16)
        a2_sb = sb("a2_sb", [128, 172], BF16)
        ix_sb = sb("ix_sb", [128, 32], I32)
        eps_sb = sb("eps_sb", [1, 1], F32)
        stats = sb("stats", [1, 4096], F32, side="right")
        mean_a = stats[0:1, 0:1024]
        msq_a = stats[0:1, 1024:2048]
        vsd_a = stats[0:1, 2048:3072]
        rstd_a = stats[0:1, 3072:4096]
        mean_b = stats[0:1, 0:1024]
        rstd_b = stats[0:1, 3072:4096]

        R = Rec()
        dG = Sem(nc, ES, "dG")
        gW = [Sem(nc, ES, "gWA"), Sem(nc, ES, "gWB")]
        gX = [Sem(nc, ES, f"gX{i}") for i in range(4)]
        gAM = Sem(nc, ES, "gAM")
        sCS = Sem(nc, ES, "sCS")
        sRL = [Sem(nc, ES, "sRL0"), Sem(nc, ES, "sRL1")]
        sST = Sem(nc, ES, "sST")
        sAO = Sem(nc, ES, "sAO")
        sOut = Sem(nc, ES, "sOut")
        sTR = Sem(nc, ES, "sTR")
        sQS = Sem(nc, ES, "sQS")
        gWQ = Sem(nc, ES, "gWQ")
        sSTb = Sem(nc, ES, "sSTb")
        sOutb = Sem(nc, ES, "sOutb")
        tP = Sem(nc, ES, "tP")
        tV = Sem(nc, ES, "tV")
        tS = Sem(nc, ES, "tS")
        cC = [Sem(nc, ES, f"cC{g}") for g in range(3)]

        def mm(out, l, r, st, sp, inc=False):
            fn = lambda b, o=out, l=l, r=r, st=st, sp=sp: b.matmul(o, l, r, start=st, stop=sp)
            R.on("tensor", _i(fn, tP) if inc else fn)

        R.on("gpsimd", lambda b: b.memset(ones_c[:, :], 1.0))
        R.on("gpsimd", lambda b: b.memset(onesr_f[:, :], 1.0))
        R.on("gpsimd", lambda b: b.memset(onesr_b[:, :], 1.0))
        R.on("gpsimd", lambda b: b.memset(eps_sb[:, :], 1e-5))
        for o, i in ((ident_bf, idh), (a1_sb, a1R), (a3_sb, a3R), (a2_sb, a2R), (ix_sb, idxg)):
            R.on("gpsimd", _i(lambda b, o=o, i=i: b.dma_start(out=o[:, :], in_=i[:, :]), dG, 16))
        R.on("sync", _i(lambda b: b.dma_start(out=m0_sb[:, :], in_=m0h[:, :]), sCS, 16))
        R.wait("gpsimd", dG, dG.v)
        R.wait("sync", sCS, sCS.v)
        with nc.Block() as blk:
            R.flush(blk)

        esKVQ = contextlib.ExitStack()
        sbK = lambda n, sh, dt: esKVQ.enter_context(nc.sbuf_tensor(n, sh, dt))
        kT_sb = sbK("kT_sb", [128, KVH * S], BF16)
        v_sb = sbK("v_sb", [128, KVH * 32 * 129], BF16)
        coll_q = sbK("coll_q", [1, 64], F32)
        coll_k = sbK("coll_k", [1, 16], F32)
        bias_sb = sbK("bias_sb", [128, 8], F32)

        with contextlib.ExitStack() as esA:
            sbA = lambda n, sh, dt: esA.enter_context(nc.sbuf_tensor(n, sh, dt))
            xt_h = [sbA("xtA", [128, 16 * 512], BF16), sbA("xtB", [128, 16 * 512], BF16)]
            wbuf = sbA("wbuf", [128, 32 * 1024], BF16)
            wkv_buf = sbA("wkv_buf", [128, 32 * 512], BF16)
            cos_sl = sbA("cos_sl", [128, 512], F32)
            sin_sl = sbA("sin_sl", [128, 512], F32)
            swap_sb = sbA("swap_sb", [128, 512], F32)
            t1_sb = sbA("t1_sb", [128, 512], F32)
            t2_sb = sbA("t2_sb", [128, 512], F32)
            sq_sb = [sbA("sq_sb0", [128, 512], BF16), sbA("sq_sb1", [128, 512], BF16)]
            vt_bf = sbA("vt_bf", [128, 512], BF16)
            qstage = [sbA("qstage0", [128, 512], BF16), sbA("qstage1", [128, 512], BF16)]
            pA = [esA.enter_context(nc.psum_tensor(f"pA{i}", [128, 512], F32)) for i in range(4)]
            pnorm = [esA.enter_context(nc.psum_tensor(f"pnorm{i}", [1, 512], F32)) for i in range(2)]
            pstA = esA.enter_context(nc.psum_tensor("pstA", [128, 512], BF16))

            for kv in range(KVH):
                for st in range(32):
                    base = (kv * 32 + st) * 129 + 128
                    R.on("gpsimd", lambda b, base=base: b.memset(v_sb[:, base:base + 1], 1.0))

            trk = {"pA": [(tP, 0)] * 4, "pnorm": [0, 0], "sq": [0, 0], "swap": 0,
                   "vt": 0, "pst": 0, "xt": [0, 0], "cs": 0,
                   "qs": [(0, 0), (0, 0)], "qsn": 0}

            def qkv_pass(pass_id, waves):
                if pass_id == 0:
                    R.wait("gpsimd", dG, dG.v)
                    for q4 in range(2):
                        for off, w in ((0, wkT), (256, wvT)):
                            o = wkv_buf[:, q4 * 16 * 512:(q4 + 1) * 16 * 512].rearrange(
                                "p (a x) -> p a x", x=512)[:, :, off:off + 256]
                            i = w[q4 * 2048:(q4 + 1) * 2048, :].rearrange("(a p) o -> p a o", p=128)
                            R.on("gpsimd", _i(lambda b, o=o, i=i: b.dma_start(out=o, in_=i), dG, 16))
                wload = dG.v

                for tb in range(8):
                    xval = [0] * 4
                    for half in (0, 1):
                        R.wait("gpsimd", tP, trk["xt"][half])
                        for qq in (0, 1):
                            q4 = 2 * half + qq
                            o = xt_h[half][:, qq * 8 * 512:(qq + 1) * 8 * 512].rearrange("p (a t) -> p a t", t=512)
                            i = xt[q4 * 1024:(q4 + 1) * 1024, 512 * tb:512 * (tb + 1)].rearrange("(a p) t -> p a t", p=128)
                            R.on("gpsimd", _i(lambda b, o=o, i=i: b.dma_start(out=o, in_=i), gX[q4], 16))
                            xval[q4] = gX[q4].v
                    if pass_id == 0:
                        a0 = tb * 4
                        o = wbuf[:, a0 * 1024:(a0 + 4) * 1024].rearrange("p (a o) -> p a o", o=1024)
                        i = wqT[a0 * 128:(a0 + 4) * 128, :].rearrange("(a p) o -> p a o", p=128)
                        R.on("gpsimd", _i(lambda b, o=o, i=i: b.dma_start(out=o, in_=i), gWQ, 16))
                    R.wait("sync", tV, trk["cs"])
                    R.on("sync", _i(lambda b, tb=tb: b.dma_start(out=cos_sl[:, :], in_=cosT[:, 512 * tb:512 * (tb + 1)]), sCS, 16))
                    R.on("sync", _i(lambda b, tb=tb: b.dma_start(out=sin_sl[:, :], in_=sinsT[:, 512 * tb:512 * (tb + 1)]), sCS, 16))
                    csval = sCS.v

                    for wi, wave in enumerate(waves):
                        last_wave = (wi == len(waves) - 1)
                        pbase = 2 * (wi % 2)
                        for ti in range(len(wave)):
                            gs, gv = trk["pA"][pbase + ti]
                            R.wait("tensor", gs, gv)
                        if wi == 0 and tb == 0:
                            R.wait("tensor", dG, wload)
                            if pass_id == 1:
                                R.wait("tensor", gWQ, gWQ.v)

                        def wsl(kind, ol, a):
                            if pass_id == 0:
                                off = 0 if kind == "k" else 256
                                return wkv_buf[:, a * 512 + off + 128 * ol:a * 512 + off + 128 * ol + 128]
                            return wbuf[:, a * 1024 + 128 * ol:a * 1024 + 128 * ol + 128]

                        for a in range(32):
                            if a % 8 == 0:
                                R.wait("tensor", gX[a // 8], xval[a // 8])
                            for ti, (kind, ol) in enumerate(wave):
                                do_inc = (ti == len(wave) - 1) and (a in (15, 31))
                                mm(pA[pbase + ti][:, :], wsl(kind, ol, a),
                                   xt_h[a // 16][:, (a % 16) * 512:(a % 16 + 1) * 512],
                                   a == 0, a == 31, inc=do_inc)
                            if a == 15 and last_wave:
                                trk["xt"][0] = tP.v
                        if last_wave:
                            trk["xt"][1] = tP.v
                        wave_done = tP.v

                        for ti, (kind, ol) in enumerate(wave):
                            pi = pbase + ti
                            pe_done = wave_done
                            if kind in ("q", "k"):
                                if kind == "q":
                                    qs = trk["qsn"] % 2
                                    trk["qsn"] += 1
                                    dst = qstage[qs][:, :]
                                    sq_done, dma_done = trk["qs"][qs]
                                    R.wait("vector", tS, sq_done)
                                    R.wait("vector", sQS, dma_done)
                                else:
                                    dst = kT_sb[:, ol * S + 512 * tb:ol * S + 512 * (tb + 1)]
                                R.wait("scalar", tP, pe_done)
                                R.wait("scalar", tV, trk["swap"])
                                R.on("scalar", lambda b, p=pA[pi]: b.copy(swap_sb[0:64, :], p[64:128, :]))
                                R.on("scalar", _i(lambda b, p=pA[pi]: b.copy(swap_sb[64:128, :], p[0:64, :]), tS))
                                sw = tS.v
                                R.wait("vector", tP, pe_done)
                                R.wait("vector", tS, sw)
                                R.wait("vector", sCS, csval)
                                R.on("vector", lambda b, p=pA[pi]: b.tensor_mul(t1_sb[:, :], p[:, :], cos_sl[:, :]))
                                R.on("vector", lambda b: b.tensor_mul(t2_sb[:, :], swap_sb[:, :], sin_sl[:, :]))
                                R.on("vector", _i(lambda b, d=dst: b.tensor_add(d, t1_sb[:, :], t2_sb[:, :]), tV))
                                rope = tV.v
                                trk["pA"][pi] = (tV, rope)
                                trk["swap"] = rope
                                trk["cs"] = rope
                                if kind == "q":
                                    R.wait("sync", tV, rope)
                                    fn = lambda b, ol=ol, tb=tb, qs=qs: b.dma_start(
                                        out=qt_hbm[128 * ol:128 * (ol + 1), 512 * tb:512 * (tb + 1)],
                                        in_=qstage[qs][:, :])
                                    R.on("sync", _i(fn, sQS, 16))
                                ns = ti % 2
                                R.wait("scalar", tV, rope)
                                R.wait("scalar", tP, trk["sq"][ns])
                                R.on("scalar", _i(lambda b, d=dst, ns=ns: b.activation(sq_sb[ns][:, :], d, AF.Square), tS))
                                sqv = tS.v
                                if kind == "q":
                                    trk["qs"][qs] = (sqv, sQS.v)
                                R.wait("tensor", tS, sqv)
                                R.wait("tensor", tV, trk["pnorm"][ns])
                                mm(pnorm[ns][:, :], ones_c[:, :], sq_sb[ns][:, :], True, True, inc=True)
                                trk["sq"][ns] = tP.v
                                coll = coll_q if kind == "q" else coll_k
                                ci = ol * 8 + tb
                                R.wait("vector", tP, tP.v)
                                R.on("vector", _i(lambda b, c=coll, ci=ci, ns=ns: b.reduce_max(c[0:1, ci:ci + 1], pnorm[ns][0:1, :], axis=AX), tV))
                                trk["pnorm"][ns] = tV.v
                            else:
                                R.wait("scalar", tP, pe_done)
                                R.wait("scalar", tP, trk["vt"])
                                R.on("scalar", _i(lambda b, p=pA[pi]: b.copy(vt_bf[:, :], p[:, :]), tS))
                                ve = tS.v
                                trk["pA"][pi] = (tS, ve)
                                R.wait("tensor", tS, ve)
                                R.wait("tensor", tS, trk["pst"])
                                for c in range(4):
                                    fn = lambda b, c=c: b.transpose(pstA[:, 128 * c:128 * (c + 1)], vt_bf[:, 128 * c:128 * (c + 1)], ident_bf[:, :])
                                    R.on("tensor", _i(fn, tP) if c == 3 else fn)
                                trv = tP.v
                                trk["vt"] = trv
                                R.wait("scalar", tP, trv)
                                for c in range(4):
                                    base = (ol * 32 + tb * 4 + c) * 129
                                    fn = lambda b, base=base, c=c: b.copy(v_sb[:, base:base + 128], pstA[:, 128 * c:128 * (c + 1)])
                                    R.on("scalar", _i(fn, tS) if c == 3 else fn)
                                trk["pst"] = tS.v

            qkv_pass(0, [[("k", 0), ("k", 1)], [("v", 0), ("v", 1)]])
            with nc.Block() as blk:
                R.flush(blk)
            qkv_pass(1, [[("q", 0), ("q", 1)], [("q", 2), ("q", 3)],
                         [("q", 4), ("q", 5)], [("q", 6), ("q", 7)]])
            with nc.Block() as blk:
                R.flush(blk)
        qspill_all = sQS.v

        with contextlib.ExitStack() as esT:
            sbT = lambda n, sh, dt: esT.enter_context(nc.sbuf_tensor(n, sh, dt))
            P_sb = [sbT(f"P{i}", [128, 1024], BF16) for i in range(2)]
            aoTok = [sbT(f"aoTok{i}", [128, 512], BF16) for i in range(2)]
            aoT_qh = [sbT(f"aoT_qh{i}", [128, S], BF16) for i in range(2)]
            rec_sb = sbT("rec_sb", [128, 4], F32)
            qld = [sbT("qld0", [128, 512], BF16), sbT("qld1", [128, 512], BF16)]
            t11 = sbT("t11", [1, 4], F32)
            b16 = sbT("b16", [1, 8], BF16)

            if debug:
                R.wait("gpsimd", sQS, qspill_all)
                R.on("gpsimd", _i(lambda b: b.dma_start(out=dbgq[:, :].rearrange("p (h t) -> p h t", t=S), in_=qt_hbm[:, :].rearrange("(h p) t -> p h t", p=128)), dG, 16))
                R.on("gpsimd", _i(lambda b: b.dma_start(out=dbgk[:, :], in_=kT_sb[:, :]), dG, 16))
                R.on("gpsimd", _i(lambda b: b.dma_start(out=dbgv[:, :], in_=v_sb[:, :]), dG, 16))

            with contextlib.ExitStack() as esP:
                pb = esP.enter_context(nc.psum_tensor("pb", [128, 512], F32))
                pb_copy = 0
                for qh in range(QH):
                    kv = qh // 4
                    R.on("vector", lambda b, qh=qh: b.reduce_max(t11[0:1, 0:1], coll_q[0:1, qh * 8:qh * 8 + 8], axis=AX))
                    R.on("vector", lambda b, kv=kv: b.reduce_max(t11[0:1, 1:2], coll_k[0:1, kv * 8:kv * 8 + 8], axis=AX))
                    R.on("vector", _i(lambda b: b.tensor_mul(t11[0:1, 2:3], t11[0:1, 0:1], t11[0:1, 1:2]), tV))
                    R.wait("scalar", tV, tV.v)
                    R.on("scalar", _i(lambda b: b.sqrt(t11[0:1, 3:4], t11[0:1, 2:3]), tS))
                    R.wait("scalar", tS, tS.v)
                    R.on("scalar", _i(lambda b, qh=qh: b.mul(b16[0:1, qh:qh + 1], t11[0:1, 3:4], -SCALE), tS))
                    R.wait("tensor", tS, tS.v)
                    if pb_copy:
                        R.wait("tensor", tS, pb_copy)
                    mm(pb[:, 0:1], onesr_b[:, :], b16[0:1, qh:qh + 1], True, True, inc=True)
                    R.wait("scalar", tP, tP.v)
                    R.on("scalar", _i(lambda b, qh=qh: b.copy(bias_sb[:, qh:qh + 1], pb[:, 0:1]), tS))
                    pb_copy = tS.v
                with nc.Block() as blk:
                    R.flush(blk)

            S2 = [esT.enter_context(nc.psum_tensor(f"S2{i}", [128, 1024], F32)) for i in range(2)]
            AOq = [esT.enter_context(nc.psum_tensor(f"AOq{i}", [128, 512], F32)) for i in range(4)]

            if debug:
                R.wait("sync", tS, tS.v)
                R.on("sync", _i(lambda b: b.dma_start(out=dbgbias[:, :], in_=bias_sb[:, :]), sCS, 16))
                R.on("sync", _i(lambda b: b.dma_start(out=dbgcoll[0:1, 0:64], in_=coll_q[:, :]), sCS, 16))
                R.on("sync", _i(lambda b: b.dma_start(out=dbgcoll[0:1, 64:80], in_=coll_k[:, :]), sCS, 16))

            s2_free = [0, 0]
            p_free = [0, 0]
            evac_prev = 0
            trdma = 0
            trdma_slot = [0, 0]
            spill_v = 0
            spill_slot = [0, 0]
            qld_used = [0, 0]
            qld_val = {}

            def load_q(n, first=False):
                qh_, qb_ = n // 8, n % 8
                slot = n % 2
                R.wait("sync", tP, qld_used[slot])
                if first:
                    R.wait("sync", sQS, qspill_all)
                fn = lambda b, qh_=qh_, qb_=qb_, slot=slot: b.dma_start(
                    out=qld[slot][:, :], in_=qt_hbm[128 * qh_:128 * (qh_ + 1), 512 * qb_:512 * (qb_ + 1)])
                R.on("sync", _i(fn, sRL[slot], 16))
                qld_val[n] = sRL[slot].v

            load_q(0, first=True)
            load_q(1)
            for qh in range(QH):
                kv = qh // 4
                if qh > 1:
                    R.wait("sync", sAO, spill_slot[qh % 2])
                for qb in range(8):
                    n_it = qh * 8 + qb
                    qslot = n_it % 2
                    nb = (4 * qb + 4) // 2
                    sc_gate = {}
                    R.wait("tensor", sRL[n_it % 2], qld_val[n_it])

                    def emit_scores(kb, qh=qh, qb=qb, kv=kv, qslot=qslot):
                        slot = kb % 2
                        partial = [(kk, 128 * (2 * kb + kk - 4 * qb)) for kk in range(2)
                                   if 2 * kb + kk >= 4 * qb]
                        R.wait("tensor", tS, s2_free[slot])
                        for kk in range(2):
                            k = 2 * kb + kk
                            d = 128 * (k - 4 * qb) if k >= 4 * qb else 0
                            mm(S2[slot][:, 512 * kk + d:512 * (kk + 1)],
                               kT_sb[:, kv * S + 128 * k:kv * S + 128 * (k + 1)],
                               qld[qslot][:, d:512], True, True, inc=(kk == 1))
                        gate = (tP, tP.v)
                        if partial:
                            R.wait("vector", tP, tP.v)
                            for n, (kk, d) in enumerate(partial):
                                fn = lambda b, kk=kk, d=d, slot=slot: b.tensor_tensor(
                                    S2[slot][:, 512 * kk + d:512 * kk + d + 128],
                                    S2[slot][:, 512 * kk + d:512 * kk + d + 128],
                                    m0_sb[:, :], ALU.add)
                                R.on("vector", _i(fn, tV) if n == len(partial) - 1 else fn)
                            gate = (tV, tV.v)
                        sc_gate[kb] = (gate, partial)

                    emit_scores(0)
                    for kb in range(nb):
                        slot = kb % 2
                        if kb + 1 < nb:
                            emit_scores(kb + 1)
                        (gs, gv), partial = sc_gate[kb]
                        R.wait("scalar", gs, gv)
                        R.wait("scalar", tP, p_free[slot])
                        R.on("scalar", _i(lambda b, slot=slot, qh=qh: b.activation(
                            P_sb[slot][:, 0:1024], S2[slot][:, 0:1024], AF.Exp,
                            bias=bias_sb[:, qh:qh + 1], scale=SCALE), tS))
                        s2_free[slot] = tS.v
                        R.wait("tensor", tS, tS.v)
                        if kb == 0:
                            R.wait("tensor", tV, evac_prev)
                        avl = [(kk, c) for kk in range(2)
                               for c in range(max(0, 2 * kb + kk - 4 * qb), 4)]
                        for n, (kk, c) in enumerate(avl):
                            k = 2 * kb + kk
                            mm(AOq[c][:, 0:129],
                               P_sb[slot][:, 512 * kk + 128 * c:512 * kk + 128 * (c + 1)],
                               v_sb[:, (kv * 32 + k) * 129:(kv * 32 + k) * 129 + 129],
                               k == 0, k == 4 * qb + c, inc=(n == len(avl) - 1))
                        p_free[slot] = tP.v
                    tok = n_it % 2
                    R.wait("vector", tP, tP.v)
                    R.wait("vector", sTR, trdma_slot[tok])
                    for c in range(4):
                        fn = lambda b, c=c: b.reciprocal(rec_sb[:, c:c + 1], AOq[c][:, 128:129])
                        R.on("vector", _i(fn, tV) if c == 3 else fn)
                    R.wait("vector", tV, tV.v)
                    for c in range(4):
                        fn = lambda b, c=c, tok=tok: b.tensor_scalar_mul(
                            aoTok[tok][:, 128 * c:128 * (c + 1)], AOq[c][:, 0:128], rec_sb[:, c:c + 1])
                        R.on("vector", _i(fn, tV) if c == 3 else fn)
                    evac = tV.v
                    evac_prev = evac
                    R.wait("sync", tV, evac)
                    for c in range(4):
                        fn = lambda b, c=c, qb=qb, qh=qh, tok=tok: b.dma_start_transpose(
                            aoT_qh[qh % 2][:, 512 * qb + 128 * c:512 * qb + 128 * (c + 1)],
                            aoTok[tok][:, 128 * c:128 * (c + 1)])
                        R.on("sync", _i(fn, sTR, 16))
                    trdma = sTR.v
                    trdma_slot[tok] = trdma
                    qld_used[qslot] = tP.v
                    if n_it + 2 < 64:
                        load_q(n_it + 2)
                R.wait("sync", sTR, trdma)
                fn = lambda b, qh=qh: b.dma_start(
                    out=ag_in[128 * qh:128 * (qh + 1), :], in_=aoT_qh[qh % 2][:, :])
                R.on("sync", _i(fn, sAO, 16))
                spill_v = sAO.v
                spill_slot[qh % 2] = spill_v
                if qh in (3, 6, 7):
                    g = {3: 0, 6: 1, 7: 2}[qh]
                    lo, hi = {0: (0, 512), 1: (512, 896), 2: (896, 1024)}[g]
                    R.wait("gpsimd", sAO, spill_v)
                    R.on("gpsimd", _i(lambda b, g=g, lo=lo, hi=hi: b.collective_compute(
                        "AllGather", ALU.bypass, replica_groups=RG,
                        ins=[ag_in[lo:hi, :]],
                        outs=[ag_out_g[g].ap().opt()]), cC[g]))
            with nc.Block() as blk:
                R.flush(blk)
        esKVQ.close()

        with contextlib.ExitStack() as esW:
            sbW = lambda n, sh, dt: esW.enter_context(nc.sbuf_tensor(n, sh, dt))
            ao_mov = sbW("ao_mov", [128, 32 * 1024], BF16)
            wo_st = [sbW(f"wo_st{i}", [128, 32 * 128], BF16) for i in range(2)]
            xres = [sbW(f"xres{i}", [128, 1024], F32) for i in range(2)]
            res_row = [sbW(f"resrow{i}", [128, 1024], F32) for i in range(2)]
            resbf = [sbW(f"resbf{i}", [128, 512], BF16) for i in range(4)]
            sqbf = [sbW(f"sqbf{i}", [128, 512], BF16) for i in range(4)]
            pwo = [esW.enter_context(nc.psum_tensor(f"pwo{i}", [128, 512], F32)) for i in range(2)]
            pms = [esW.enter_context(nc.psum_tensor(f"pms{i}", [1, 512], F32)) for i in range(2)]
            pmq = [esW.enter_context(nc.psum_tensor(f"pmq{i}", [1, 512], F32)) for i in range(2)]

            wo_used = [0, 0]
            wo_ld = {}

            def load_wo(j):
                slot = j % 2
                R.wait("gpsimd", tP, wo_used[slot])
                for k in range(2):
                    o = wo_st[slot][:, k * 16 * 128:(k + 1) * 16 * 128].rearrange("p (a j) -> p a j", j=128)
                    i = woT[2048 * k:2048 * (k + 1), 128 * j:128 * (j + 1)].rearrange("(a p) j -> p a j", p=128)
                    R.on("gpsimd", _i(lambda b, o=o, i=i: b.dma_start(out=o, in_=i), gW[slot], 16))
                wo_ld[j] = gW[slot].v

            xr_used = [0, 0]
            xr_ld = {}

            def load_xres(j):
                slot = j % 2
                R.wait("sync", tV, xr_used[slot])
                R.on("sync", _i(lambda b, j=j, slot=slot: b.dma_start(
                    out=xres[slot][:, :], in_=xts[128 * j:128 * (j + 1), :]), sRL[slot], 16))
                xr_ld[j] = sRL[slot].v

            load_wo(0)
            load_wo(1)
            load_xres(0)
            load_xres(1)
            for g in range(3):
                R.wait("gpsimd", cC[g], 1)
                for a in range(32):
                    if (0 if a % 8 < 4 else (1 if a % 8 < 7 else 2)) != g:
                        continue
                    fn = lambda b, a=a, g=g: b.indirect_dma_start(
                        ao_mov[:, 1024 * a:1024 * (a + 1)], None, ag_out_g[g][0:128, 0:1024],
                        IndirectOffsetOnAxis(ap=ix_sb[:, a:a + 1], axis=1))
                    R.on("gpsimd", _i(fn, gAM, 16))
            am_v = gAM.v
            if debug:
                R.on("gpsimd", _i(lambda b: b.dma_start(out=dbgao[:, :], in_=ag_in[0:1024, :]), dG, 16))
                R.wait("gpsimd", gAM, am_v)
                R.on("gpsimd", _i(lambda b: b.dma_start(out=dbggo[:, :], in_=ao_mov[:, :]), dG, 16))


            pwo_free = [0, 0]
            resbf_free = [0, 0, 0, 0]
            rr_store = [(sST, 0), (sSTb, 0)]
            res_add = {}
            pending_mom = []

            def flush_mom(keep=0):
                while len(pending_mom) > keep:
                    j_, tb_, sqd_ = pending_mom.pop(0)
                    si = 2 * tb_ + j_ % 2
                    R.wait("tensor", tS, sqd_)
                    mm(pms[tb_][:, :], ones_c[:, :], resbf[si][:, :], j_ == 0, j_ == 31, inc=False)
                    mm(pmq[tb_][:, :], ones_c[:, :], sqbf[si][:, :], j_ == 0, j_ == 31, inc=True)
                    resbf_free[si] = tP.v

            for j in range(32):
                slot = j % 2
                R.wait("tensor", gW[slot], wo_ld[j])
                if j == 0:
                    R.wait("tensor", gAM, am_v)
                flush_mom(keep=2)
                R.wait("tensor", tV, pwo_free[0])
                R.wait("tensor", tV, pwo_free[1])
                for a in range(32):
                    for tb in range(2):
                        mm(pwo[tb][:, :], wo_st[slot][:, 128 * a:128 * (a + 1)],
                           ao_mov[:, 1024 * a + 512 * tb:1024 * a + 512 * (tb + 1)],
                           a == 0, a == 31, inc=(a == 31 and tb == 1))
                wo_all = tP.v
                for tb in range(2):
                    wo_done = wo_all
                    R.wait("vector", tP, wo_done)
                    R.wait("vector", sRL[slot], xr_ld[j])
                    if j >= 2:
                        R.wait("vector", rr_store[slot][0], rr_store[slot][1])
                    R.on("vector", _i(lambda b, tb=tb, slot=slot: b.tensor_add(
                        res_row[slot][:, 512 * tb:512 * (tb + 1)], pwo[tb][:, :],
                        xres[slot][:, 512 * tb:512 * (tb + 1)]), tV))
                    pwo_free[tb] = tV.v
                    ra = tV.v
                    si = 2 * tb + j % 2
                    R.wait("scalar", tV, ra)
                    R.wait("scalar", tP, resbf_free[si])
                    R.on("scalar", lambda b, si=si, tb=tb, slot=slot: b.copy(
                        resbf[si][:, :], res_row[slot][:, 512 * tb:512 * (tb + 1)]))
                    R.on("scalar", _i(lambda b, si=si, tb=tb, slot=slot: b.activation(
                        sqbf[si][:, :], res_row[slot][:, 512 * tb:512 * (tb + 1)], AF.Square), tS))
                    sqd = tS.v
                    pending_mom.append((j, tb, sqd))
                    res_add[(j, tb)] = ra
                wo_used[slot] = tP.v
                xr_used[slot] = tV.v
                R.wait("sync", tV, res_add[(j, 1)])
                ssem = sST if j % 2 == 0 else sSTb
                R.on("sync", _i(lambda b, j=j, slot=slot: b.dma_start(
                    out=res_hbm[128 * j:128 * (j + 1), :], in_=res_row[slot][:, :]), ssem, 16))
                rr_store[slot] = (ssem, ssem.v)
                if j + 2 < 32:
                    load_wo(j + 2)
                    load_xres(j + 2)
            res_store_all = (sST.v, sSTb.v)
            flush_mom()
            mom_done = tP.v
            R.wait("vector", tP, mom_done)
            for tb in range(2):
                R.on("vector", lambda b, tb=tb: b.tensor_scalar_mul(mean_a[0:1, 512 * tb:512 * (tb + 1)], pms[tb][0:1, :], 1.0 / 4096.0))
                R.on("vector", _i(lambda b, tb=tb: b.tensor_scalar_mul(msq_a[0:1, 512 * tb:512 * (tb + 1)], pmq[tb][0:1, :], 1.0 / 4096.0), tV))
            R.wait("scalar", tV, tV.v)
            R.on("scalar", _i(lambda b: b.activation(vsd_a, mean_a, AF.Square), tS))
            R.wait("vector", tS, tS.v)
            R.on("vector", _i(lambda b: b.tensor_sub(msq_a, msq_a, vsd_a), tV))
            R.wait("scalar", tV, tV.v)
            R.on("scalar", _i(lambda b: b.activation(vsd_a, msq_a, AF.Sqrt, bias=eps_sb[0:1, 0:1]), tS))
            R.wait("vector", tS, tS.v)
            R.on("vector", _i(lambda b: b.reciprocal(rstd_a, vsd_a), tV))
            stats1_done = tV.v
            with nc.Block() as blk:
                R.flush(blk)

        esXF = contextlib.ExitStack()
        xf_bf = esXF.enter_context(nc.sbuf_tensor("xf_bf", [128, 8 * 1024], BF16))
        yl1_bf = esXF.enter_context(nc.sbuf_tensor("yl1_bf", [2, 1024], BF16))
        yl3_bf = esXF.enter_context(nc.sbuf_tensor("yl3_bf", [2, 1024], BF16))
        with contextlib.ExitStack() as esL:
            sbL = lambda n, sh, dt: esL.enter_context(nc.sbuf_tensor(n, sh, dt))
            xf_acc = sbL("xf_acc", [128, 8 * 1024], F32)
            meanB = [sbL(f"meanB{i}", [128, 512], F32) for i in range(2)]
            rstdB = [sbL(f"rstdB{i}", [128, 512], F32) for i in range(2)]
            rrow = [sbL(f"rrow{i}", [128, 1024], F32) for i in range(2)]
            tmp32 = sbL("tmp32", [128, 512], F32)
            y32 = [sbL(f"y32_{i}", [128, 512], F32) for i in range(2)]
            ybf = [sbL(f"ybf{i}", [128, 512], BF16) for i in range(2)]
            pbc = [esL.enter_context(nc.psum_tensor(f"pbc{i}", [128, 512], F32)) for i in range(2)]
            pyl = [esL.enter_context(nc.psum_tensor(f"pyl{i}", [2, 512], F32)) for i in range(4)]

            R.wait("tensor", tV, stats1_done)
            bc_done = 0
            for tb in range(2):
                if bc_done:
                    R.wait("tensor", tV, bc_done)
                mm(pbc[0][:, :], onesr_f[:, :], mean_a[0:1, 512 * tb:512 * (tb + 1)], True, True, inc=False)
                mm(pbc[1][:, :], onesr_f[:, :], rstd_a[0:1, 512 * tb:512 * (tb + 1)], True, True, inc=True)
                R.wait("vector", tP, tP.v)
                R.on("vector", lambda b, tb=tb: b.tensor_copy(meanB[tb][:, :], pbc[0][:, :]))
                R.on("vector", _i(lambda b, tb=tb: b.tensor_copy(rstdB[tb][:, :], pbc[1][:, :]), tV))
                bc_done = tV.v

            rl_used = [0, 0]
            rl_ld = {}

            def load_res(j, src, first_wait=0):
                slot = j % 2
                R.wait("sync", tV, rl_used[slot])
                if first_wait:
                    R.wait("sync", sST, first_wait[0])
                    R.wait("sync", sSTb, first_wait[1])
                R.on("sync", _i(lambda b, j=j, slot=slot, src=src: b.dma_start(
                    out=rrow[slot][:, :], in_=src[128 * j:128 * (j + 1), :]), sRL[slot], 16))
                rl_ld[j] = sRL[slot].v

            load_res(0, res_hbm, first_wait=res_store_all)
            load_res(1, res_hbm)
            ybf_used = [0, 0]
            y32_used = [0, 0]
            for j in range(32):
                slot = j % 2
                for tb in range(2):
                    pi = (2 * j + tb) % 2
                    R.wait("vector", sRL[slot], rl_ld[j])
                    if y32_used[pi]:
                        R.wait("vector", tS, y32_used[pi])
                    R.on("vector", lambda b, slot=slot, tb=tb: b.tensor_sub(
                        tmp32[:, :], rrow[slot][:, 512 * tb:512 * (tb + 1)], meanB[tb][:, :]))
                    R.on("vector", lambda b, pi=pi, tb=tb: b.tensor_mul(y32[pi][:, :], tmp32[:, :], rstdB[tb][:, :]))
                    dxf = xf_acc[:, (j % 8) * 1024 + 512 * tb:(j % 8) * 1024 + 512 * (tb + 1)]
                    if j < 8:
                        R.on("vector", _i(lambda b, d=dxf, pi=pi: b.tensor_copy(d, y32[pi][:, :]), tV))
                    else:
                        R.on("vector", _i(lambda b, d=dxf, pi=pi: b.tensor_add(d, d, y32[pi][:, :]), tV))
                    yv = tV.v
                    R.wait("scalar", tV, yv)
                    R.wait("scalar", tP, ybf_used[pi])
                    R.on("scalar", _i(lambda b, pi=pi: b.copy(ybf[pi][:, :], y32[pi][:, :]), tS))
                    y32_used[pi] = tS.v
                    R.wait("tensor", tS, tS.v)
                    mm(pyl[tb][:, :], a1_sb[:, 2 * j:2 * j + 2], ybf[pi][:, :], j == 0, j == 31, inc=False)
                    mm(pyl[2 + tb][:, :], a3_sb[:, 2 * j:2 * j + 2], ybf[pi][:, :], j == 0, j == 31, inc=True)
                    ybf_used[pi] = tP.v
                rl_used[slot] = tV.v
                if j + 2 < 32:
                    load_res(j + 2, res_hbm)
            xf_done = tV.v
            yl_done = tP.v
            R.wait("scalar", tP, yl_done)
            for tb in range(2):
                R.on("scalar", lambda b, tb=tb: b.mul(yl1_bf[0:2, 512 * tb:512 * (tb + 1)], pyl[tb][0:2, :], 0.5))
                R.on("scalar", lambda b, tb=tb: b.mul(yl3_bf[0:2, 512 * tb:512 * (tb + 1)], pyl[2 + tb][0:2, :], 0.5))
            R.wait("scalar", tV, xf_done)
            R.on("scalar", _i(lambda b: b.copy(xf_bf[:, :], xf_acc[:, :]), tS))
            if debug:
                R.wait("sync", tS, tS.v)
                R.on("sync", _i(lambda b: b.dma_start(out=dbgst[:, :], in_=stats[:, :]), sCS, 16))
                R.wait("gpsimd", tS, tS.v)
                R.on("gpsimd", _i(lambda b: b.dma_start(out=dbgxf[:, :], in_=xf_bf[:, :]), dG, 16))
                R.on("gpsimd", _i(lambda b: b.dma_start(out=dbgyl[0:2, 0:1024], in_=yl1_bf[:, :]), dG, 16))
                R.on("gpsimd", _i(lambda b: b.dma_start(out=dbgyl[0:2, 1024:2048], in_=yl3_bf[:, :]), dG, 16))
            with nc.Block() as blk:
                R.flush(blk)

        gl_bf = sb("gl_bf", [2, 1024], BF16, side="right")
        esGF = contextlib.ExitStack()
        gf_acc = esGF.enter_context(nc.sbuf_tensor("gf_acc", [128, 22 * 1024], F32, side="right"))
        with contextlib.ExitStack() as esF:
            sbF = lambda n, sh, dt: esF.enter_context(nc.sbuf_tensor(n, sh, dt))
            b1_sb = sbF("b1_sb", [2, HFF], BF16)
            b3_sb = sbF("b3_sb", [2, HFF], BF16)
            R.wait("gpsimd", dG, dG.v)
            R.on("gpsimd", _i(lambda b: b.dma_start(out=b1_sb[:, :], in_=b1T[:, :]), dG, 16))
            R.on("gpsimd", _i(lambda b: b.dma_start(out=b3_sb[:, :], in_=b3T[:, :]), dG, 16))
            bload = dG.v
            R.on("vector", _i(lambda b: b.memset(gf_acc[:, :], 0.0), tV))
            gf_zero = tV.v
            w1_st = [sbF(f"w1st{i}", [128, 8 * 256], BF16) for i in range(2)]
            w3_st = [sbF(f"w3st{i}", [128, 8 * 256], BF16) for i in range(2)]
            silu32 = [sbF(f"silu{i}", [128, 512], F32) for i in range(2)]
            g32 = [sbF(f"g32_{i}", [128, 512], F32) for i in range(2)]
            gbf = [sbF(f"gbf{i}", [128, 512], BF16) for i in range(3)]
            tmpal = sbF("tmpal", [128, 512], F32)
            ph = [esF.enter_context(nc.psum_tensor(f"ph{i}", [128, 512], F32)) for i in range(4)]
            pgl = [esF.enter_context(nc.psum_tensor(f"pgl{i}", [2, 512], F32)) for i in range(2)]

            w_used = [0, 0]
            w_ld = {}

            def load_w13(hs):
                slot = hs % 2
                R.wait("gpsimd", tP, w_used[slot])
                for w, st in ((w1qT, w1_st), (w3qT, w3_st)):
                    o = st[slot][:, :].rearrange("p (a h) -> p a h", h=256)
                    i = w[:, 256 * hs:256 * (hs + 1)].rearrange("(a p) h -> p a h", p=128)
                    R.on("gpsimd", _i(lambda b, o=o, i=i: b.dma_start(out=o, in_=i), gW[slot], 16))
                w_ld[hs] = gW[slot].v

            load_w13(0)
            load_w13(1)
            pending_gl = []

            def flush_gl(keep=0):
                while len(pending_gl) > keep:
                    h_, tb_, gi_, gv_ = pending_gl.pop(0)
                    R.wait("tensor", tS, gv_)
                    mm(pgl[tb_][:, :], a2_sb[:, 2 * h_:2 * h_ + 2], gbf[gi_][:, :],
                       h_ == 0, h_ == NHT - 1, inc=True)
                    gbf_used[gi_] = tP.v
            ph_free = [0, 0, 0, 0]
            ph_sread = [0, 0]
            ph_vread = [0, 0]
            silu_used = [0, 0]
            g32_used = [0, 0]
            gbf_used = [0, 0, 0]
            it = 0
            for hs in range(43):
                slot = hs % 2
                for ht in range(2):
                    h = 2 * hs + ht
                    for tb in range(2):
                        pi = it % 2
                        it3 = it
                        it += 1
                        R.wait("tensor", gW[slot], w_ld[hs])
                        if h == 0 and tb == 0:
                            R.wait("tensor", dG, bload)
                        flush_gl(keep=2)
                        R.wait("tensor", tS, ph_sread[pi])
                        R.wait("tensor", tV, ph_vread[pi])
                        for a in range(8):
                            mm(ph[2 * pi][:, :], w1_st[slot][:, a * 256 + 128 * ht:a * 256 + 128 * (ht + 1)],
                               xf_bf[:, 1024 * a + 512 * tb:1024 * a + 512 * (tb + 1)], a == 0, False)
                        mm(ph[2 * pi][:, :], b1_sb[0:2, 128 * h:128 * (h + 1)],
                           yl1_bf[0:2, 512 * tb:512 * (tb + 1)], False, True, inc=False)
                        for a in range(8):
                            mm(ph[2 * pi + 1][:, :], w3_st[slot][:, a * 256 + 128 * ht:a * 256 + 128 * (ht + 1)],
                               xf_bf[:, 1024 * a + 512 * tb:1024 * a + 512 * (tb + 1)], a == 0, False)
                        mm(ph[2 * pi + 1][:, :], b3_sb[0:2, 128 * h:128 * (h + 1)],
                           yl3_bf[0:2, 512 * tb:512 * (tb + 1)], False, True, inc=True)
                        hmm = tP.v
                        R.wait("scalar", tP, hmm)
                        if silu_used[pi]:
                            R.wait("scalar", tV, silu_used[pi])
                        R.on("scalar", _i(lambda b, pi=pi: b.activation(silu32[pi][:, :], ph[2 * pi][:, :], AF.Silu), tS))
                        sl = tS.v
                        ph_sread[pi] = sl
                        R.wait("vector", tS, sl)
                        if g32_used[pi]:
                            R.wait("vector", tS, g32_used[pi])
                        R.on("vector", _i(lambda b, pi=pi: b.tensor_mul(g32[pi][:, :], silu32[pi][:, :], ph[2 * pi + 1][:, :]), tV))
                        gm = tV.v
                        ph_vread[pi] = gm
                        silu_used[pi] = gm
                        R.wait("scalar", tV, gm)
                        gi = it3 % 3
                        R.wait("scalar", tP, gbf_used[gi])
                        segs = fold_segments(h)
                        need_tmp = [(so, L, dr) for (so, L, dr) in segs if so % 128 != dr % 128]
                        R.on("scalar", _i(lambda b, pi=pi, gi=gi: b.copy(gbf[gi][:, :], g32[pi][:, :]), tS))
                        gbf_done = tS.v
                        g32_used[pi] = max(gbf_done, g32_used[pi])
                        pending_gl.append((h, tb, gi, gbf_done))
                        if h == 0 and tb == 0:
                            R.wait("vector", tV, gf_zero)
                        for so, L, dr in need_tmp:
                            doff = dr % 128
                            R.on("vector", lambda b, so=so, L=L, pi=pi, doff=doff: b.tensor_copy(
                                tmpal[doff:doff + L, :], g32[pi][so:so + L, :]))
                        for n, (so, L, dr) in enumerate(segs):
                            dt_ = dr // 128
                            doff = dr % 128
                            dsl = gf_acc[doff:doff + L, 1024 * dt_ + 512 * tb:1024 * dt_ + 512 * (tb + 1)]
                            if so % 128 == doff:
                                fn = lambda b, d=dsl, pi=pi, so=so, L=L: b.tensor_add(d, d, g32[pi][so:so + L, :])
                            else:
                                fn = lambda b, d=dsl, L=L, doff=doff: b.tensor_add(d, d, tmpal[doff:doff + L, :])
                            R.on("vector", _i(fn, tV) if n == len(segs) - 1 else fn)
                w_used[slot] = tP.v
                if hs + 2 < 43:
                    load_w13(hs + 2)
            flush_gl()
            gl_done = tP.v
            fold_done = tV.v
            R.wait("scalar", tP, gl_done)
            for tb in range(2):
                fn = lambda b, tb=tb: b.mul(gl_bf[0:2, 512 * tb:512 * (tb + 1)], pgl[tb][0:2, :], 0.5)
                R.on("scalar", _i(fn, tS) if tb == 1 else fn)
            with nc.Block() as blk:
                R.flush(blk)

        esXF.close()
        gf_bf = sb("gf_bf", [128, 22 * 1024], BF16)
        with contextlib.ExitStack() as esW2:
            sbW2 = lambda n, sh, dt: esW2.enter_context(nc.sbuf_tensor(n, sh, dt))
            b2_sb = sbW2("b2_sb", [2, D], BF16)
            R.wait("gpsimd", dG, dG.v)
            R.on("gpsimd", _i(lambda b: b.dma_start(out=b2_sb[:, :], in_=b2T[:, :]), dG, 16))
            b2load = dG.v
            R.wait("scalar", tV, fold_done)
            R.on("scalar", _i(lambda b: b.copy(gf_bf[:, :], gf_acc[:, :]), tS))
            gfb_done = tS.v
            if debug:
                R.wait("gpsimd", tS, tS.v)
                R.on("gpsimd", _i(lambda b: b.dma_start(out=dbggf[:, :], in_=gf_bf[:, :]), dG, 16))
            with nc.Block() as blk:
                R.flush(blk)
            esGF.close()
            w2_st = [sbW2(f"w2st{i}", [128, 22 * 128], BF16) for i in range(2)]
            meanB = [sbW2(f"meanB2_{i}", [128, 512], F32) for i in range(2)]
            rstdB = [sbW2(f"rstdB2_{i}", [128, 512], F32) for i in range(2)]
            rrow = [sbW2(f"rrow2_{i}", [128, 1024], F32) for i in range(2)]
            r2row = [sbW2(f"r2row{i}", [128, 1024], F32) for i in range(2)]
            tmp32 = sbW2("tmp32b", [128, 512], F32)
            y32 = sbW2("y32b", [128, 512], F32)
            r2bf = [sbW2(f"r2bf{i}", [128, 512], BF16) for i in range(4)]
            sq2bf = [sbW2(f"sq2bf{i}", [128, 512], BF16) for i in range(4)]
            pw2 = [esW2.enter_context(nc.psum_tensor(f"pw2_{i}", [128, 512], F32)) for i in range(2)]
            pm2s = [esW2.enter_context(nc.psum_tensor(f"pm2s{i}", [1, 512], F32)) for i in range(2)]
            pm2q = [esW2.enter_context(nc.psum_tensor(f"pm2q{i}", [1, 512], F32)) for i in range(2)]
            pbc2 = [esW2.enter_context(nc.psum_tensor(f"pbc2_{i}", [128, 512], F32)) for i in range(2)]

            bc_done = 0
            for tb in range(2):
                if bc_done:
                    R.wait("tensor", tV, bc_done)
                mm(pbc2[0][:, :], onesr_f[:, :], mean_a[0:1, 512 * tb:512 * (tb + 1)], True, True, inc=False)
                mm(pbc2[1][:, :], onesr_f[:, :], rstd_a[0:1, 512 * tb:512 * (tb + 1)], True, True, inc=True)
                R.wait("vector", tP, tP.v)
                R.on("vector", lambda b, tb=tb: b.tensor_copy(meanB[tb][:, :], pbc2[0][:, :]))
                R.on("vector", _i(lambda b, tb=tb: b.tensor_copy(rstdB[tb][:, :], pbc2[1][:, :]), tV))
                bc_done = tV.v

            w2_used = [0, 0]
            w2_ld = {}

            def load_w2(j):
                slot = j % 2
                R.wait("gpsimd", tP, w2_used[slot])
                o1 = w2_st[slot][:, 0:21 * 128].rearrange("p (a j) -> p a j", j=128)
                i1 = w2qT[0:2688, 128 * j:128 * (j + 1)].rearrange("(a p) j -> p a j", p=128)
                R.on("gpsimd", _i(lambda b, o=o1, i=i1: b.dma_start(out=o, in_=i), gW[slot], 16))
                o2 = w2_st[slot][0:64, 21 * 128:22 * 128]
                i2 = w2qT[2688:2752, 128 * j:128 * (j + 1)]
                R.on("gpsimd", _i(lambda b, o=o2, i=i2: b.dma_start(out=o, in_=i), gW[slot], 16))
                w2_ld[j] = gW[slot].v

            rl_used = [0, 0]
            rl_ld = {}

            def load_res2(j, src, first_wait=0):
                slot = j % 2
                R.wait("sync", tV, rl_used[slot])
                if first_wait:
                    R.wait("sync", sST, first_wait)
                R.on("sync", _i(lambda b, j=j, slot=slot, src=src: b.dma_start(
                    out=rrow[slot][:, :], in_=src[128 * j:128 * (j + 1), :]), sRL[slot], 16))
                rl_ld[j] = sRL[slot].v

            load_w2(0)
            load_w2(1)
            load_res2(0, res_hbm)
            load_res2(1, res_hbm)
            pw2_free = [0, 0]
            r2bf_free = [0, 0, 0, 0]
            r2_store = [(sST, 0), (sSTb, 0)]
            pending_m2 = []

            def flush_m2(keep=0):
                while len(pending_m2) > keep:
                    j_, tb_, sqd_ = pending_m2.pop(0)
                    si = 2 * tb_ + j_ % 2
                    R.wait("tensor", tS, sqd_)
                    mm(pm2s[tb_][:, :], ones_c[:, :], r2bf[si][:, :], j_ == 0, j_ == 31, inc=False)
                    mm(pm2q[tb_][:, :], ones_c[:, :], sq2bf[si][:, :], j_ == 0, j_ == 31, inc=True)
                    r2bf_free[si] = tP.v

            for j in range(32):
                slot = j % 2
                r2a = 0
                R.wait("tensor", gW[slot], w2_ld[j])
                if j == 0:
                    R.wait("tensor", tS, gfb_done)
                    R.wait("tensor", dG, b2load)
                flush_m2(keep=2)
                R.wait("tensor", tV, pw2_free[0])
                R.wait("tensor", tV, pw2_free[1])
                for a in range(22):
                    pd = 128 if a < 21 else 64
                    for tb in range(2):
                        mm(pw2[tb][:, :], w2_st[slot][0:pd, 128 * a:128 * (a + 1)],
                           gf_bf[0:pd, 1024 * a + 512 * tb:1024 * a + 512 * (tb + 1)], a == 0, False)
                for tb in range(2):
                    mm(pw2[tb][:, :], b2_sb[0:2, 128 * j:128 * (j + 1)],
                       gl_bf[0:2, 512 * tb:512 * (tb + 1)], False, True, inc=(tb == 1))
                w2all = tP.v
                for tb in range(2):
                    w2done = w2all
                    R.wait("vector", tP, w2done)
                    R.wait("vector", sRL[slot], rl_ld[j])
                    if j >= 2:
                        R.wait("vector", r2_store[slot][0], r2_store[slot][1])
                    R.on("vector", lambda b, slot=slot, tb=tb: b.tensor_sub(
                        tmp32[:, :], rrow[slot][:, 512 * tb:512 * (tb + 1)], meanB[tb][:, :]))
                    R.on("vector", lambda b, tb=tb: b.tensor_mul(y32[:, :], tmp32[:, :], rstdB[tb][:, :]))
                    R.on("vector", _i(lambda b, slot=slot, tb=tb: b.tensor_add(
                        r2row[slot][:, 512 * tb:512 * (tb + 1)], y32[:, :], pw2[tb][:, :]), tV))
                    pw2_free[tb] = tV.v
                    r2a = tV.v
                    si = 2 * tb + j % 2
                    R.wait("scalar", tV, r2a)
                    R.wait("scalar", tP, r2bf_free[si])
                    R.on("scalar", lambda b, si=si, tb=tb, slot=slot: b.copy(
                        r2bf[si][:, :], r2row[slot][:, 512 * tb:512 * (tb + 1)]))
                    R.on("scalar", _i(lambda b, si=si, tb=tb, slot=slot: b.activation(
                        sq2bf[si][:, :], r2row[slot][:, 512 * tb:512 * (tb + 1)], AF.Square), tS))
                    pending_m2.append((j, tb, tS.v))
                w2_used[slot] = tP.v
                rl_used[slot] = tV.v
                R.wait("sync", tV, r2a)
                ssem = sST if j % 2 == 0 else sSTb
                R.on("sync", _i(lambda b, j=j, slot=slot: b.dma_start(
                    out=res2_hbm[128 * j:128 * (j + 1), :], in_=r2row[slot][:, :]), ssem, 16))
                r2_store[slot] = (ssem, ssem.v)
                if j + 2 < 32:
                    load_w2(j + 2)
                    load_res2(j + 2, res_hbm)
            res2_store_all = (sST.v, sSTb.v)
            flush_m2()
            mom_done = tP.v
            R.wait("vector", tP, mom_done)
            for tb in range(2):
                R.on("vector", lambda b, tb=tb: b.tensor_scalar_mul(mean_b[0:1, 512 * tb:512 * (tb + 1)], pm2s[tb][0:1, :], 1.0 / 4096.0))
                R.on("vector", _i(lambda b, tb=tb: b.tensor_scalar_mul(msq_a[0:1, 512 * tb:512 * (tb + 1)], pm2q[tb][0:1, :], 1.0 / 4096.0), tV))
            R.wait("scalar", tV, tV.v)
            R.on("scalar", _i(lambda b: b.activation(vsd_a, mean_b, AF.Square), tS))
            R.wait("vector", tS, tS.v)
            R.on("vector", _i(lambda b: b.tensor_sub(msq_a, msq_a, vsd_a), tV))
            R.wait("scalar", tV, tV.v)
            R.on("scalar", _i(lambda b: b.activation(vsd_a, msq_a, AF.Sqrt, bias=eps_sb[0:1, 0:1]), tS))
            R.wait("vector", tS, tS.v)
            R.on("vector", _i(lambda b: b.reciprocal(rstd_b, vsd_a), tV))
            stats2_done = tV.v
            with nc.Block() as blk:
                R.flush(blk)

        with contextlib.ExitStack() as esO:
            sbO = lambda n, sh, dt: esO.enter_context(nc.sbuf_tensor(n, sh, dt))
            meanB = [sbO(f"meanBo{i}", [128, 512], F32) for i in range(2)]
            rstdB = [sbO(f"rstdBo{i}", [128, 512], F32) for i in range(2)]
            rrow = [sbO(f"rrowo{i}", [128, 1024], F32) for i in range(2)]
            orow = [sbO(f"orow{i}", [128, 1024], F32) for i in range(2)]
            tmp32 = sbO("tmp32o", [128, 512], F32)
            pbcO = [esO.enter_context(nc.psum_tensor(f"pbcO{i}", [128, 512], F32)) for i in range(2)]

            R.wait("tensor", tV, stats2_done)
            bc_done = 0
            for tb in range(2):
                if bc_done:
                    R.wait("tensor", tV, bc_done)
                mm(pbcO[0][:, :], onesr_f[:, :], mean_b[0:1, 512 * tb:512 * (tb + 1)], True, True, inc=False)
                mm(pbcO[1][:, :], onesr_f[:, :], rstd_b[0:1, 512 * tb:512 * (tb + 1)], True, True, inc=True)
                R.wait("vector", tP, tP.v)
                R.on("vector", lambda b, tb=tb: b.tensor_copy(meanB[tb][:, :], pbcO[0][:, :]))
                R.on("vector", _i(lambda b, tb=tb: b.tensor_copy(rstdB[tb][:, :], pbcO[1][:, :]), tV))
                bc_done = tV.v

            rl_used = [0, 0]
            rl_ld = {}

            def load_r2o(j, first_wait=0):
                slot = j % 2
                R.wait("sync", tV, rl_used[slot])
                if first_wait:
                    R.wait("sync", sST, first_wait[0])
                    R.wait("sync", sSTb, first_wait[1])
                R.on("sync", _i(lambda b, j=j, slot=slot: b.dma_start(
                    out=rrow[slot][:, :], in_=res2_hbm[128 * j:128 * (j + 1), :]), sRL[slot], 16))
                rl_ld[j] = sRL[slot].v

            load_r2o(0, first_wait=res2_store_all)
            load_r2o(1)
            o_store = [(sOut, 0), (sOutb, 0)]
            for j in range(32):
                slot = j % 2
                R.wait("vector", sRL[slot], rl_ld[j])
                if j >= 2:
                    R.wait("vector", o_store[slot][0], o_store[slot][1])
                for tb in range(2):
                    R.on("vector", lambda b, slot=slot, tb=tb: b.tensor_sub(
                        tmp32[:, :], rrow[slot][:, 512 * tb:512 * (tb + 1)], meanB[tb][:, :]))
                    fn = lambda b, slot=slot, tb=tb: b.tensor_mul(
                        orow[slot][:, 512 * tb:512 * (tb + 1)], tmp32[:, :], rstdB[tb][:, :])
                    R.on("vector", _i(fn, tV) if tb == 1 else fn)
                ov = tV.v
                rl_used[slot] = ov
                R.wait("sync", tV, ov)
                osem = sOut if j % 2 == 0 else sOutb
                R.on("sync", _i(lambda b, j=j, slot=slot: b.dma_start(
                    out=outT[128 * j:128 * (j + 1), :], in_=orow[slot][:, :]), osem, 16))
                o_store[slot] = (osem, osem.v)
                if j + 2 < 32:
                    load_r2o(j + 2)
            R.wait("gpsimd", sOut, sOut.v)
            R.wait("gpsimd", sOutb, sOutb.v)
            R.wait("gpsimd", sST, sST.v)
            R.wait("gpsimd", sSTb, sSTb.v)
            R.wait("gpsimd", dG, dG.v)
            with nc.Block() as blk:
                R.flush(blk)
    return nc


_NC_CACHE = {}


def _get_nc():
    if "nc" not in _NC_CACHE:
        _NC_CACHE["nc"] = build_nc()
    return _NC_CACHE["nc"]


def _host_constants():
    inv = 1.0 / (10000.0 ** (np.arange(0, 128, 2, dtype=np.float64) / 128.0))
    pos = np.arange(S, dtype=np.float64)
    fr = pos[:, None] * inv[None, :]
    emb = np.concatenate([fr, fr], axis=1)
    cosT_h = np.ascontiguousarray(np.cos(emb).T.astype(np.float32))
    sinT = np.sin(emb).T.astype(np.float32)
    sinsT_h = sinT.copy()
    sinsT_h[0:64] *= -1.0
    sinsT_h = np.ascontiguousarray(sinsT_h)
    m0 = np.where(np.arange(128)[:, None] <= np.arange(128)[None, :], 0.0, -1e9).astype(np.float32)
    ident = np.eye(128, dtype=np.float32)
    return cosT_h, sinsT_h, m0, ident


def _swz(a):
    n = a.shape[0] // 128
    r = a.shape[1]
    return np.ascontiguousarray(a.reshape(n, 128, r).transpose(1, 0, 2).reshape(128, n * r))


def kernel(L0_attn_hidden_states, L0_attn_Wq, L0_attn_Wk, L0_attn_Wv, L0_attn_Wo,
           L0_ffn_w1q, L0_ffn_a1, L0_ffn_b1, L0_ffn_w3q, L0_ffn_a3, L0_ffn_b3,
           L0_ffn_w2q, L0_ffn_a2, L0_ffn_b2):
    X = np.asarray(L0_attn_hidden_states, dtype=np.float32)
    Wq = np.asarray(L0_attn_Wq, np.float32)
    Wk = np.asarray(L0_attn_Wk, np.float32)
    Wv = np.asarray(L0_attn_Wv, np.float32)
    Wo = np.asarray(L0_attn_Wo, np.float32)
    w1q = np.asarray(L0_ffn_w1q, np.float32)
    w3q = np.asarray(L0_ffn_w3q, np.float32)
    w2q = np.asarray(L0_ffn_w2q, np.float32)
    a1 = np.asarray(L0_ffn_a1, np.float32)
    a3 = np.asarray(L0_ffn_a3, np.float32)
    a2 = np.asarray(L0_ffn_a2, np.float32)
    b1 = np.asarray(L0_ffn_b1, np.float32)
    b3 = np.asarray(L0_ffn_b3, np.float32)
    b2 = np.asarray(L0_ffn_b2, np.float32)

    cosT_h, sinsT_h, m0, ident = _host_constants()
    XT = [np.ascontiguousarray(X[b].T) for b in range(B)]
    woT_h = np.ascontiguousarray(Wo.T)
    w1qT_h = np.ascontiguousarray(w1q.T)
    w3qT_h = np.ascontiguousarray(w3q.T)
    w2qT_h = np.ascontiguousarray(w2q.T)
    a1R = _swz(np.ascontiguousarray(a1.T))
    a3R = _swz(np.ascontiguousarray(a3.T))
    a2R = _swz(np.ascontiguousarray(a2.T))
    b1T_h = np.ascontiguousarray(b1.T)
    b3T_h = np.ascontiguousarray(b3.T)
    b2T_h = np.ascontiguousarray(b2.T)

    in_maps = []
    for c in range(8):
        b = c // 4
        r = c % 4
        a_ = np.arange(32)[None, :]
        hl = a_ % 8
        row = np.where(hl < 4, 512 * (a_ // 8) + 128 * hl,
                       np.where(hl < 7, 384 * (a_ // 8) + 128 * (hl - 4),
                                128 * (a_ // 8)))
        idx = (row + np.arange(128)[:, None]) * S + 1024 * r
        in_maps.append({
            "xt": XT[b],
            "xts": np.ascontiguousarray(XT[b][:, 1024 * r:1024 * (r + 1)]),
            "wqT": np.ascontiguousarray(Wq[1024 * r:1024 * (r + 1), :].T),
            "wkT": np.ascontiguousarray(Wk[256 * r:256 * (r + 1), :].T),
            "wvT": np.ascontiguousarray(Wv[256 * r:256 * (r + 1), :].T),
            "woT": woT_h,
            "w1qT": w1qT_h, "w3qT": w3qT_h, "w2qT": w2qT_h,
            "a1R": a1R, "a3R": a3R, "a2R": a2R,
            "b1T": b1T_h, "b3T": b3T_h, "b2T": b2T_h,
            "cosT": cosT_h, "sinsT": sinsT_h,
            "m0": m0, "ident": ident,
            "idxg": idx.astype(np.int32),
        })
    nc = _get_nc()
    res = run_bass_kernel_spmd(nc, in_maps, core_ids=list(range(8)))
    out = np.empty((B, S, D), dtype=np.float32)
    for c in range(8):
        b = c // 4
        r = c % 4
        out[b, 1024 * r:1024 * (r + 1), :] = res.results[c]["outT"].T
    return out


# version 14
# speedup vs baseline: 1.0219x; 1.0219x over previous
import contextlib
import numpy as np
import ml_dtypes
import concourse.bass as bass
import concourse.mybir as mybir
from concourse.bass import IndirectOffsetOnAxis
from concourse.bass_utils import run_bass_kernel_spmd

F32 = mybir.dt.float32
BF16 = mybir.dt.bfloat16
I32 = mybir.dt.int32
AF = mybir.ActivationFunctionType
ALU = mybir.AluOpType
AX = mybir.AxisListType.X

B, S, D = 2, 4096, 4096
QH, KVH, TS = 8, 2, 1024
HFF, DQ, D2 = 11008, 1024, 2752
NHT = 86
SCALE = 1.0 / float(np.sqrt(128.0))
RG = [[0, 1, 2, 3], [4, 5, 6, 7]]
ENGS = ("sync", "tensor", "vector", "scalar", "gpsimd")


class Rec:
    def __init__(self):
        self.q = {e: [] for e in ENGS}

    def on(self, eng, fn):
        self.q[eng].append(fn)

    def wait(self, eng, sem, val):
        if val > 0:
            self.q[eng].append(lambda b, s=sem.h, v=val: b.wait_ge(s, v))

    def flush(self, blk):
        for e in ENGS:
            fns = self.q[e]
            if fns:
                getattr(blk, e)(lambda bld, fns=tuple(fns): [f(bld) for f in fns])
        self.q = {e: [] for e in ENGS}


class Sem:
    def __init__(self, nc, es, name):
        self.h = es.enter_context(nc.semaphore(name))
        self.v = 0


def _i(fn, sem, k=1):
    sem.v += k
    return lambda b, fn=fn, s=sem.h, k=k: fn(b).then_inc(s, k)


def fold_segments(h):
    segs = []
    i = 0
    while i < 128:
        g = 128 * h + i
        dst = g % D2
        L = min(128 - i, D2 - dst, 128 - (dst % 128))
        segs.append((i, L, dst))
        i += L
    return segs


def build_nc(debug=False):
    nc = bass.Bass("TRN2", target_bir_lowering=False)

    xt = nc.dram_tensor("xt", [D, S], F32, kind="ExternalInput").ap()
    xts = nc.dram_tensor("xts", [D, TS], F32, kind="ExternalInput").ap()
    wqT = nc.dram_tensor("wqT", [D, 1024], F32, kind="ExternalInput").ap()
    wkT = nc.dram_tensor("wkT", [D, 256], F32, kind="ExternalInput").ap()
    wvT = nc.dram_tensor("wvT", [D, 256], F32, kind="ExternalInput").ap()
    woT = nc.dram_tensor("woT", [D, D], F32, kind="ExternalInput").ap()
    w1qT = nc.dram_tensor("w1qT", [DQ, HFF], F32, kind="ExternalInput").ap()
    w3qT = nc.dram_tensor("w3qT", [DQ, HFF], F32, kind="ExternalInput").ap()
    w2qT = nc.dram_tensor("w2qT", [D2, D], F32, kind="ExternalInput").ap()
    a1R = nc.dram_tensor("a1R", [128, 64], F32, kind="ExternalInput").ap()
    a3R = nc.dram_tensor("a3R", [128, 64], F32, kind="ExternalInput").ap()
    a2R = nc.dram_tensor("a2R", [128, 172], F32, kind="ExternalInput").ap()
    b1T = nc.dram_tensor("b1T", [2, HFF], F32, kind="ExternalInput").ap()
    b3T = nc.dram_tensor("b3T", [2, HFF], F32, kind="ExternalInput").ap()
    b2T = nc.dram_tensor("b2T", [2, D], F32, kind="ExternalInput").ap()
    cosT = nc.dram_tensor("cosT", [128, S], F32, kind="ExternalInput").ap()
    sinsT = nc.dram_tensor("sinsT", [128, S], F32, kind="ExternalInput").ap()
    m0h = nc.dram_tensor("m0", [128, 128], F32, kind="ExternalInput").ap()
    idh = nc.dram_tensor("ident", [128, 128], F32, kind="ExternalInput").ap()
    idxg = nc.dram_tensor("idxg", [128, 32], I32, kind="ExternalInput").ap()
    outT = nc.dram_tensor("outT", [D, TS], F32, kind="ExternalOutput").ap()

    rkind = "ExternalOutput" if debug else "Internal"
    res_hbm = nc.dram_tensor("res_hbm", [D, TS], F32, kind=rkind).ap()
    res2_hbm = nc.dram_tensor("res2_hbm", [D, TS], F32, kind=rkind).ap()
    qt_hbm = nc.dram_tensor("qt_hbm", [1024, S], BF16).ap()
    ag_in = nc.dram_tensor("ag_in", [1024, S], BF16)
    ag_out_g = [nc.dram_tensor("ag_out0", [2048, S], BF16), nc.dram_tensor("ag_out1", [1536, S], BF16), nc.dram_tensor("ag_out2", [512, S], BF16)]
    if debug:
        dbgq = nc.dram_tensor("dq", [128, QH * S], F32, kind="ExternalOutput").ap()
        dbgk = nc.dram_tensor("dk", [128, KVH * S], F32, kind="ExternalOutput").ap()
        dbgv = nc.dram_tensor("dv", [128, KVH * 32 * 129], F32, kind="ExternalOutput").ap()
        dbgao = nc.dram_tensor("dao", [1024, S], F32, kind="ExternalOutput").ap()
        dbggo = nc.dram_tensor("dgo", [128, 32 * 1024], F32, kind="ExternalOutput").ap()
        dbgst = nc.dram_tensor("dst_o", [1, 6144], F32, kind="ExternalOutput").ap()
        dbgxf = nc.dram_tensor("dxf", [128, 8 * 1024], F32, kind="ExternalOutput").ap()
        dbggf = nc.dram_tensor("dgf", [128, 22 * 1024], F32, kind="ExternalOutput").ap()
        dbgyl = nc.dram_tensor("dyl", [2, 2048], F32, kind="ExternalOutput").ap()
        dbgbias = nc.dram_tensor("dbias", [128, 8], F32, kind="ExternalOutput").ap()
        dbgcoll = nc.dram_tensor("dcoll", [1, 80], F32, kind="ExternalOutput").ap()
        dbgS = nc.dram_tensor("dS", [128, 4096], F32, kind="ExternalOutput").ap()
        dbgAO = nc.dram_tensor("dAO", [128, 2560], F32, kind="ExternalOutput").ap()

    ES = contextlib.ExitStack()
    with ES:
        sb = lambda n, sh, dt, **kw: ES.enter_context(nc.sbuf_tensor(n, sh, dt, **kw))
        ident_bf = sb("ident_bf", [128, 128], BF16)
        m0_sb = sb("m0_sb", [128, 128], F32)
        ones_c = sb("ones_c", [128, 1], BF16)
        onesr_f = sb("onesr_f", [1, 128], F32)
        onesr_b = sb("onesr_b", [1, 128], BF16)
        a1_sb = sb("a1_sb", [128, 64], BF16)
        a3_sb = sb("a3_sb", [128, 64], BF16)
        a2_sb = sb("a2_sb", [128, 172], BF16)
        ix_sb = sb("ix_sb", [128, 32], I32)
        eps_sb = sb("eps_sb", [1, 1], F32)
        stats = sb("stats", [1, 4096], F32, side="right")
        mean_a = stats[0:1, 0:1024]
        msq_a = stats[0:1, 1024:2048]
        vsd_a = stats[0:1, 2048:3072]
        rstd_a = stats[0:1, 3072:4096]
        mean_b = stats[0:1, 0:1024]
        rstd_b = stats[0:1, 3072:4096]

        R = Rec()
        dG = Sem(nc, ES, "dG")
        gW = [Sem(nc, ES, "gWA"), Sem(nc, ES, "gWB")]
        gX = [Sem(nc, ES, f"gX{i}") for i in range(4)]
        gAM = Sem(nc, ES, "gAM")
        sCS = Sem(nc, ES, "sCS")
        sRL = [Sem(nc, ES, "sRL0"), Sem(nc, ES, "sRL1")]
        sST = Sem(nc, ES, "sST")
        sAO = Sem(nc, ES, "sAO")
        sOut = Sem(nc, ES, "sOut")
        sTR = Sem(nc, ES, "sTR")
        sQS = Sem(nc, ES, "sQS")
        gWQ = Sem(nc, ES, "gWQ")
        sSTb = Sem(nc, ES, "sSTb")
        sOutb = Sem(nc, ES, "sOutb")
        tP = Sem(nc, ES, "tP")
        tV = Sem(nc, ES, "tV")
        tS = Sem(nc, ES, "tS")
        cC = [Sem(nc, ES, f"cC{g}") for g in range(3)]

        def mm(out, l, r, st, sp, inc=False):
            fn = lambda b, o=out, l=l, r=r, st=st, sp=sp: b.matmul(o, l, r, start=st, stop=sp)
            R.on("tensor", _i(fn, tP) if inc else fn)

        R.on("gpsimd", lambda b: b.memset(ones_c[:, :], 1.0))
        R.on("gpsimd", lambda b: b.memset(onesr_f[:, :], 1.0))
        R.on("gpsimd", lambda b: b.memset(onesr_b[:, :], 1.0))
        R.on("gpsimd", lambda b: b.memset(eps_sb[:, :], 1e-5))
        for o, i in ((ident_bf, idh), (a1_sb, a1R), (a3_sb, a3R), (a2_sb, a2R), (ix_sb, idxg)):
            R.on("gpsimd", _i(lambda b, o=o, i=i: b.dma_start(out=o[:, :], in_=i[:, :]), dG, 16))
        R.on("sync", _i(lambda b: b.dma_start(out=m0_sb[:, :], in_=m0h[:, :]), sCS, 16))
        R.wait("gpsimd", dG, dG.v)
        R.wait("sync", sCS, sCS.v)
        with nc.Block() as blk:
            R.flush(blk)

        esKVQ = contextlib.ExitStack()
        sbK = lambda n, sh, dt: esKVQ.enter_context(nc.sbuf_tensor(n, sh, dt))
        kT_sb = sbK("kT_sb", [128, KVH * S], BF16)
        v_sb = sbK("v_sb", [128, KVH * 32 * 129], BF16)
        coll_q = sbK("coll_q", [1, 64], F32)
        coll_k = sbK("coll_k", [1, 16], F32)
        bias_sb = sbK("bias_sb", [128, 8], F32)

        with contextlib.ExitStack() as esA:
            sbA = lambda n, sh, dt: esA.enter_context(nc.sbuf_tensor(n, sh, dt))
            xt_h = [sbA("xtA", [128, 16 * 512], BF16), sbA("xtB", [128, 16 * 512], BF16)]
            wbuf = sbA("wbuf", [128, 32 * 1024], BF16)
            wkv_buf = sbA("wkv_buf", [128, 32 * 512], BF16)
            cos_sl = sbA("cos_sl", [128, 512], F32)
            sin_sl = sbA("sin_sl", [128, 512], F32)
            swap_sb = sbA("swap_sb", [128, 512], F32)
            t1_sb = sbA("t1_sb", [128, 512], F32)
            t2_sb = sbA("t2_sb", [128, 512], F32)
            sq_sb = [sbA("sq_sb0", [128, 512], BF16), sbA("sq_sb1", [128, 512], BF16)]
            vt_bf = sbA("vt_bf", [128, 512], BF16)
            qstage = [sbA("qstage0", [128, 512], BF16), sbA("qstage1", [128, 512], BF16)]
            pA = [esA.enter_context(nc.psum_tensor(f"pA{i}", [128, 512], F32)) for i in range(4)]
            pnorm = [esA.enter_context(nc.psum_tensor(f"pnorm{i}", [1, 512], F32)) for i in range(2)]
            pstA = esA.enter_context(nc.psum_tensor("pstA", [128, 512], BF16))

            for kv in range(KVH):
                for st in range(32):
                    base = (kv * 32 + st) * 129 + 128
                    R.on("gpsimd", lambda b, base=base: b.memset(v_sb[:, base:base + 1], 1.0))

            trk = {"pA": [(tP, 0)] * 4, "pnorm": [0, 0], "sq": [0, 0], "swap": 0,
                   "vt": 0, "pst": 0, "xt": [0, 0], "cs": 0,
                   "qs": [(0, 0), (0, 0)], "qsn": 0}

            def qkv_pass(pass_id, waves):
                if pass_id == 0:
                    R.wait("gpsimd", dG, dG.v)
                    for q4 in range(2):
                        for off, w in ((0, wkT), (256, wvT)):
                            o = wkv_buf[:, q4 * 16 * 512:(q4 + 1) * 16 * 512].rearrange(
                                "p (a x) -> p a x", x=512)[:, :, off:off + 256]
                            i = w[q4 * 2048:(q4 + 1) * 2048, :].rearrange("(a p) o -> p a o", p=128)
                            R.on("gpsimd", _i(lambda b, o=o, i=i: b.dma_start(out=o, in_=i), dG, 16))
                wload = dG.v

                for tb in range(8):
                    xval = [0] * 4
                    for half in (0, 1):
                        R.wait("gpsimd", tP, trk["xt"][half])
                        for qq in (0, 1):
                            q4 = 2 * half + qq
                            o = xt_h[half][:, qq * 8 * 512:(qq + 1) * 8 * 512].rearrange("p (a t) -> p a t", t=512)
                            i = xt[q4 * 1024:(q4 + 1) * 1024, 512 * tb:512 * (tb + 1)].rearrange("(a p) t -> p a t", p=128)
                            R.on("gpsimd", _i(lambda b, o=o, i=i: b.dma_start(out=o, in_=i), gX[q4], 16))
                            xval[q4] = gX[q4].v
                    if pass_id == 0:
                        a0 = tb * 4
                        o = wbuf[:, a0 * 1024:(a0 + 4) * 1024].rearrange("p (a o) -> p a o", o=1024)
                        i = wqT[a0 * 128:(a0 + 4) * 128, :].rearrange("(a p) o -> p a o", p=128)
                        R.on("gpsimd", _i(lambda b, o=o, i=i: b.dma_start(out=o, in_=i), gWQ, 16))
                    R.wait("sync", tV, trk["cs"])
                    R.on("sync", _i(lambda b, tb=tb: b.dma_start(out=cos_sl[:, :], in_=cosT[:, 512 * tb:512 * (tb + 1)]), sCS, 16))
                    R.on("sync", _i(lambda b, tb=tb: b.dma_start(out=sin_sl[:, :], in_=sinsT[:, 512 * tb:512 * (tb + 1)]), sCS, 16))
                    csval = sCS.v

                    for wi, wave in enumerate(waves):
                        last_wave = (wi == len(waves) - 1)
                        pbase = 2 * (wi % 2)
                        for ti in range(len(wave)):
                            gs, gv = trk["pA"][pbase + ti]
                            R.wait("tensor", gs, gv)
                        if wi == 0 and tb == 0:
                            R.wait("tensor", dG, wload)
                            if pass_id == 1:
                                R.wait("tensor", gWQ, gWQ.v)

                        def wsl(kind, ol, a):
                            if pass_id == 0:
                                off = 0 if kind == "k" else 256
                                return wkv_buf[:, a * 512 + off + 128 * ol:a * 512 + off + 128 * ol + 128]
                            return wbuf[:, a * 1024 + 128 * ol:a * 1024 + 128 * ol + 128]

                        for a in range(32):
                            if a % 8 == 0:
                                R.wait("tensor", gX[a // 8], xval[a // 8])
                            for ti, (kind, ol) in enumerate(wave):
                                do_inc = (ti == len(wave) - 1) and (a in (15, 31))
                                mm(pA[pbase + ti][:, :], wsl(kind, ol, a),
                                   xt_h[a // 16][:, (a % 16) * 512:(a % 16 + 1) * 512],
                                   a == 0, a == 31, inc=do_inc)
                            if a == 15 and last_wave:
                                trk["xt"][0] = tP.v
                        if last_wave:
                            trk["xt"][1] = tP.v
                        wave_done = tP.v

                        for ti, (kind, ol) in enumerate(wave):
                            pi = pbase + ti
                            pe_done = wave_done
                            if kind in ("q", "k"):
                                if kind == "q":
                                    qs = trk["qsn"] % 2
                                    trk["qsn"] += 1
                                    dst = qstage[qs][:, :]
                                    sq_done, dma_done = trk["qs"][qs]
                                    R.wait("vector", tS, sq_done)
                                    R.wait("vector", sQS, dma_done)
                                else:
                                    dst = kT_sb[:, ol * S + 512 * tb:ol * S + 512 * (tb + 1)]
                                R.wait("scalar", tP, pe_done)
                                R.wait("scalar", tV, trk["swap"])
                                R.on("scalar", lambda b, p=pA[pi]: b.copy(swap_sb[0:64, :], p[64:128, :]))
                                R.on("scalar", _i(lambda b, p=pA[pi]: b.copy(swap_sb[64:128, :], p[0:64, :]), tS))
                                sw = tS.v
                                R.wait("vector", tP, pe_done)
                                R.wait("vector", tS, sw)
                                R.wait("vector", sCS, csval)
                                R.on("vector", lambda b, p=pA[pi]: b.tensor_mul(t1_sb[:, :], p[:, :], cos_sl[:, :]))
                                R.on("vector", lambda b: b.tensor_mul(t2_sb[:, :], swap_sb[:, :], sin_sl[:, :]))
                                R.on("vector", _i(lambda b, d=dst: b.tensor_add(d, t1_sb[:, :], t2_sb[:, :]), tV))
                                rope = tV.v
                                trk["pA"][pi] = (tV, rope)
                                trk["swap"] = rope
                                trk["cs"] = rope
                                if kind == "q":
                                    R.wait("sync", tV, rope)
                                    fn = lambda b, ol=ol, tb=tb, qs=qs: b.dma_start(
                                        out=qt_hbm[128 * ol:128 * (ol + 1), 512 * tb:512 * (tb + 1)],
                                        in_=qstage[qs][:, :])
                                    R.on("sync", _i(fn, sQS, 16))
                                ns = ti % 2
                                R.wait("scalar", tV, rope)
                                R.wait("scalar", tP, trk["sq"][ns])
                                R.on("scalar", _i(lambda b, d=dst, ns=ns: b.activation(sq_sb[ns][:, :], d, AF.Square), tS))
                                sqv = tS.v
                                if kind == "q":
                                    trk["qs"][qs] = (sqv, sQS.v)
                                R.wait("tensor", tS, sqv)
                                R.wait("tensor", tV, trk["pnorm"][ns])
                                mm(pnorm[ns][:, :], ones_c[:, :], sq_sb[ns][:, :], True, True, inc=True)
                                trk["sq"][ns] = tP.v
                                coll = coll_q if kind == "q" else coll_k
                                ci = ol * 8 + tb
                                R.wait("vector", tP, tP.v)
                                R.on("vector", _i(lambda b, c=coll, ci=ci, ns=ns: b.reduce_max(c[0:1, ci:ci + 1], pnorm[ns][0:1, :], axis=AX), tV))
                                trk["pnorm"][ns] = tV.v
                            else:
                                R.wait("scalar", tP, pe_done)
                                R.wait("scalar", tP, trk["vt"])
                                R.on("scalar", _i(lambda b, p=pA[pi]: b.copy(vt_bf[:, :], p[:, :]), tS))
                                ve = tS.v
                                trk["pA"][pi] = (tS, ve)
                                R.wait("tensor", tS, ve)
                                R.wait("tensor", tS, trk["pst"])
                                for c in range(4):
                                    fn = lambda b, c=c: b.transpose(pstA[:, 128 * c:128 * (c + 1)], vt_bf[:, 128 * c:128 * (c + 1)], ident_bf[:, :])
                                    R.on("tensor", _i(fn, tP) if c == 3 else fn)
                                trv = tP.v
                                trk["vt"] = trv
                                R.wait("scalar", tP, trv)
                                for c in range(4):
                                    base = (ol * 32 + tb * 4 + c) * 129
                                    fn = lambda b, base=base, c=c: b.copy(v_sb[:, base:base + 128], pstA[:, 128 * c:128 * (c + 1)])
                                    R.on("scalar", _i(fn, tS) if c == 3 else fn)
                                trk["pst"] = tS.v

            qkv_pass(0, [[("k", 0), ("k", 1)], [("v", 0), ("v", 1)]])
            with nc.Block() as blk:
                R.flush(blk)
            qkv_pass(1, [[("q", 0), ("q", 1)], [("q", 2), ("q", 3)],
                         [("q", 4), ("q", 5)], [("q", 6), ("q", 7)]])
            with nc.Block() as blk:
                R.flush(blk)
        qspill_all = sQS.v

        with contextlib.ExitStack() as esT:
            sbT = lambda n, sh, dt: esT.enter_context(nc.sbuf_tensor(n, sh, dt))
            P_sb = [sbT(f"P{i}", [128, 1024], BF16) for i in range(3)]
            aoTok = [sbT(f"aoTok{i}", [128, 512], BF16) for i in range(2)]
            aoT_qh = [sbT(f"aoT_qh{i}", [128, S], BF16) for i in range(2)]
            rec_sb = sbT("rec_sb", [128, 4], F32)
            qld = [sbT("qld0", [128, 512], BF16), sbT("qld1", [128, 512], BF16)]
            t11 = sbT("t11", [1, 4], F32)
            b16 = sbT("b16", [1, 8], BF16)

            if debug:
                R.wait("gpsimd", sQS, qspill_all)
                R.on("gpsimd", _i(lambda b: b.dma_start(out=dbgq[:, :].rearrange("p (h t) -> p h t", t=S), in_=qt_hbm[:, :].rearrange("(h p) t -> p h t", p=128)), dG, 16))
                R.on("gpsimd", _i(lambda b: b.dma_start(out=dbgk[:, :], in_=kT_sb[:, :]), dG, 16))
                R.on("gpsimd", _i(lambda b: b.dma_start(out=dbgv[:, :], in_=v_sb[:, :]), dG, 16))

            with contextlib.ExitStack() as esP:
                pb = esP.enter_context(nc.psum_tensor("pb", [128, 512], F32))
                pb_copy = 0
                for qh in range(QH):
                    kv = qh // 4
                    R.on("vector", lambda b, qh=qh: b.reduce_max(t11[0:1, 0:1], coll_q[0:1, qh * 8:qh * 8 + 8], axis=AX))
                    R.on("vector", lambda b, kv=kv: b.reduce_max(t11[0:1, 1:2], coll_k[0:1, kv * 8:kv * 8 + 8], axis=AX))
                    R.on("vector", _i(lambda b: b.tensor_mul(t11[0:1, 2:3], t11[0:1, 0:1], t11[0:1, 1:2]), tV))
                    R.wait("scalar", tV, tV.v)
                    R.on("scalar", _i(lambda b: b.sqrt(t11[0:1, 3:4], t11[0:1, 2:3]), tS))
                    R.wait("scalar", tS, tS.v)
                    R.on("scalar", _i(lambda b, qh=qh: b.mul(b16[0:1, qh:qh + 1], t11[0:1, 3:4], -SCALE), tS))
                    R.wait("tensor", tS, tS.v)
                    if pb_copy:
                        R.wait("tensor", tS, pb_copy)
                    mm(pb[:, 0:1], onesr_b[:, :], b16[0:1, qh:qh + 1], True, True, inc=True)
                    R.wait("scalar", tP, tP.v)
                    R.on("scalar", _i(lambda b, qh=qh: b.copy(bias_sb[:, qh:qh + 1], pb[:, 0:1]), tS))
                    pb_copy = tS.v
                with nc.Block() as blk:
                    R.flush(blk)

            S2 = [esT.enter_context(nc.psum_tensor(f"S2{i}", [128, 1024], F32)) for i in range(2)]
            AOq = [esT.enter_context(nc.psum_tensor(f"AOq{i}", [128, 512], F32)) for i in range(4)]

            if debug:
                R.wait("sync", tS, tS.v)
                R.on("sync", _i(lambda b: b.dma_start(out=dbgbias[:, :], in_=bias_sb[:, :]), sCS, 16))
                R.on("sync", _i(lambda b: b.dma_start(out=dbgcoll[0:1, 0:64], in_=coll_q[:, :]), sCS, 16))
                R.on("sync", _i(lambda b: b.dma_start(out=dbgcoll[0:1, 64:80], in_=coll_k[:, :]), sCS, 16))

            s2_free = [0, 0]
            p_free = [0, 0, 0]
            evac_prev = 0
            trdma = 0
            trdma_slot = [0, 0]
            pcount = [0]
            spill_v = 0
            spill_slot = [0, 0]
            qld_used = [0, 0]
            qld_val = {}

            def load_q(n, first=False):
                qh_, qb_ = n // 8, n % 8
                slot = n % 2
                R.wait("sync", tP, qld_used[slot])
                if first:
                    R.wait("sync", sQS, qspill_all)
                fn = lambda b, qh_=qh_, qb_=qb_, slot=slot: b.dma_start(
                    out=qld[slot][:, :], in_=qt_hbm[128 * qh_:128 * (qh_ + 1), 512 * qb_:512 * (qb_ + 1)])
                R.on("sync", _i(fn, sRL[slot], 16))
                qld_val[n] = sRL[slot].v

            load_q(0, first=True)
            load_q(1)
            for qh in range(QH):
                kv = qh // 4
                if qh > 1:
                    R.wait("sync", sAO, spill_slot[qh % 2])
                for qb in range(8):
                    n_it = qh * 8 + qb
                    qslot = n_it % 2
                    nb = (4 * qb + 4) // 2
                    sc_gate = {}
                    R.wait("tensor", sRL[n_it % 2], qld_val[n_it])

                    def emit_scores(kb, qh=qh, qb=qb, kv=kv, qslot=qslot):
                        slot = kb % 2
                        partial = [(kk, 128 * (2 * kb + kk - 4 * qb)) for kk in range(2)
                                   if 2 * kb + kk >= 4 * qb]
                        R.wait("tensor", tS, s2_free[slot])
                        for kk in range(2):
                            k = 2 * kb + kk
                            d = 128 * (k - 4 * qb) if k >= 4 * qb else 0
                            mm(S2[slot][:, 512 * kk + d:512 * (kk + 1)],
                               kT_sb[:, kv * S + 128 * k:kv * S + 128 * (k + 1)],
                               qld[qslot][:, d:512], True, True, inc=(kk == 1))
                        gate = (tP, tP.v)
                        if partial:
                            R.wait("vector", tP, tP.v)
                            for n, (kk, d) in enumerate(partial):
                                fn = lambda b, kk=kk, d=d, slot=slot: b.tensor_tensor(
                                    S2[slot][:, 512 * kk + d:512 * kk + d + 128],
                                    S2[slot][:, 512 * kk + d:512 * kk + d + 128],
                                    m0_sb[:, :], ALU.add)
                                R.on("vector", _i(fn, tV) if n == len(partial) - 1 else fn)
                            gate = (tV, tV.v)
                        sc_gate[kb] = (gate, partial)

                    emit_scores(0)
                    for kb in range(nb):
                        slot = kb % 2
                        if kb + 1 < nb:
                            emit_scores(kb + 1)
                        (gs, gv), partial = sc_gate[kb]
                        pslot3 = pcount[0] % 3
                        pcount[0] += 1
                        R.wait("scalar", gs, gv)
                        R.wait("scalar", tP, p_free[pslot3])
                        R.on("scalar", _i(lambda b, slot=slot, pslot3=pslot3, qh=qh: b.activation(
                            P_sb[pslot3][:, 0:1024], S2[slot][:, 0:1024], AF.Exp,
                            bias=bias_sb[:, qh:qh + 1], scale=SCALE), tS))
                        s2_free[slot] = tS.v
                        R.wait("tensor", tS, tS.v)
                        if kb == 0:
                            R.wait("tensor", tV, evac_prev)
                        avl = [(kk, c) for kk in range(2)
                               for c in range(max(0, 2 * kb + kk - 4 * qb), 4)]
                        for n, (kk, c) in enumerate(avl):
                            k = 2 * kb + kk
                            mm(AOq[c][:, 0:129],
                               P_sb[pslot3][:, 512 * kk + 128 * c:512 * kk + 128 * (c + 1)],
                               v_sb[:, (kv * 32 + k) * 129:(kv * 32 + k) * 129 + 129],
                               k == 0, k == 4 * qb + c, inc=(n == len(avl) - 1))
                        p_free[pslot3] = tP.v
                    tok = n_it % 2
                    R.wait("vector", tP, tP.v)
                    R.wait("vector", sTR, trdma_slot[tok])
                    for c in range(4):
                        fn = lambda b, c=c: b.reciprocal(rec_sb[:, c:c + 1], AOq[c][:, 128:129])
                        R.on("vector", _i(fn, tV) if c == 3 else fn)
                    R.wait("vector", tV, tV.v)
                    for c in range(4):
                        fn = lambda b, c=c, tok=tok: b.tensor_scalar_mul(
                            aoTok[tok][:, 128 * c:128 * (c + 1)], AOq[c][:, 0:128], rec_sb[:, c:c + 1])
                        R.on("vector", _i(fn, tV) if c == 3 else fn)
                    evac = tV.v
                    evac_prev = evac
                    R.wait("sync", tV, evac)
                    for c in range(4):
                        fn = lambda b, c=c, qb=qb, qh=qh, tok=tok: b.dma_start_transpose(
                            aoT_qh[qh % 2][:, 512 * qb + 128 * c:512 * qb + 128 * (c + 1)],
                            aoTok[tok][:, 128 * c:128 * (c + 1)])
                        R.on("sync", _i(fn, sTR, 16))
                    trdma = sTR.v
                    trdma_slot[tok] = trdma
                    qld_used[qslot] = tP.v
                    if n_it + 2 < 64:
                        load_q(n_it + 2)
                R.wait("sync", sTR, trdma)
                fn = lambda b, qh=qh: b.dma_start(
                    out=ag_in[128 * qh:128 * (qh + 1), :], in_=aoT_qh[qh % 2][:, :])
                R.on("sync", _i(fn, sAO, 16))
                spill_v = sAO.v
                spill_slot[qh % 2] = spill_v
                if qh in (3, 6, 7):
                    g = {3: 0, 6: 1, 7: 2}[qh]
                    lo, hi = {0: (0, 512), 1: (512, 896), 2: (896, 1024)}[g]
                    R.wait("gpsimd", sAO, spill_v)
                    R.on("gpsimd", _i(lambda b, g=g, lo=lo, hi=hi: b.collective_compute(
                        "AllGather", ALU.bypass, replica_groups=RG,
                        ins=[ag_in[lo:hi, :]],
                        outs=[ag_out_g[g].ap().opt()]), cC[g]))
            with nc.Block() as blk:
                R.flush(blk)
        esKVQ.close()

        with contextlib.ExitStack() as esW:
            sbW = lambda n, sh, dt: esW.enter_context(nc.sbuf_tensor(n, sh, dt))
            ao_mov = sbW("ao_mov", [128, 32 * 1024], BF16)
            wo_st = [sbW(f"wo_st{i}", [128, 32 * 128], BF16) for i in range(2)]
            xres = [sbW(f"xres{i}", [128, 1024], F32) for i in range(2)]
            res_row = [sbW(f"resrow{i}", [128, 1024], F32) for i in range(2)]
            resbf = [sbW(f"resbf{i}", [128, 512], BF16) for i in range(4)]
            sqbf = [sbW(f"sqbf{i}", [128, 512], BF16) for i in range(4)]
            pwo = [esW.enter_context(nc.psum_tensor(f"pwo{i}", [128, 512], F32)) for i in range(2)]
            pms = [esW.enter_context(nc.psum_tensor(f"pms{i}", [1, 512], F32)) for i in range(2)]
            pmq = [esW.enter_context(nc.psum_tensor(f"pmq{i}", [1, 512], F32)) for i in range(2)]

            wo_used = [0, 0]
            wo_ld = {}

            def load_wo(j):
                slot = j % 2
                R.wait("gpsimd", tP, wo_used[slot])
                for k in range(2):
                    o = wo_st[slot][:, k * 16 * 128:(k + 1) * 16 * 128].rearrange("p (a j) -> p a j", j=128)
                    i = woT[2048 * k:2048 * (k + 1), 128 * j:128 * (j + 1)].rearrange("(a p) j -> p a j", p=128)
                    R.on("gpsimd", _i(lambda b, o=o, i=i: b.dma_start(out=o, in_=i), gW[slot], 16))
                wo_ld[j] = gW[slot].v

            xr_used = [0, 0]
            xr_ld = {}

            def load_xres(j):
                slot = j % 2
                R.wait("sync", tV, xr_used[slot])
                R.on("sync", _i(lambda b, j=j, slot=slot: b.dma_start(
                    out=xres[slot][:, :], in_=xts[128 * j:128 * (j + 1), :]), sRL[slot], 16))
                xr_ld[j] = sRL[slot].v

            load_wo(0)
            load_wo(1)
            load_xres(0)
            load_xres(1)
            for g in range(3):
                R.wait("gpsimd", cC[g], 1)
                for a in range(32):
                    if (0 if a % 8 < 4 else (1 if a % 8 < 7 else 2)) != g:
                        continue
                    fn = lambda b, a=a, g=g: b.indirect_dma_start(
                        ao_mov[:, 1024 * a:1024 * (a + 1)], None, ag_out_g[g][0:128, 0:1024],
                        IndirectOffsetOnAxis(ap=ix_sb[:, a:a + 1], axis=1))
                    R.on("gpsimd", _i(fn, gAM, 16))
            am_v = gAM.v
            if debug:
                R.on("gpsimd", _i(lambda b: b.dma_start(out=dbgao[:, :], in_=ag_in[0:1024, :]), dG, 16))
                R.wait("gpsimd", gAM, am_v)
                R.on("gpsimd", _i(lambda b: b.dma_start(out=dbggo[:, :], in_=ao_mov[:, :]), dG, 16))


            pwo_free = [0, 0]
            resbf_free = [0, 0, 0, 0]
            rr_store = [(sST, 0), (sSTb, 0)]
            res_add = {}
            pending_mom = []

            def flush_mom(keep=0):
                while len(pending_mom) > keep:
                    j_, tb_, sqd_ = pending_mom.pop(0)
                    si = 2 * tb_ + j_ % 2
                    R.wait("tensor", tS, sqd_)
                    mm(pms[tb_][:, :], ones_c[:, :], resbf[si][:, :], j_ == 0, j_ == 31, inc=False)
                    mm(pmq[tb_][:, :], ones_c[:, :], sqbf[si][:, :], j_ == 0, j_ == 31, inc=True)
                    resbf_free[si] = tP.v

            for j in range(32):
                slot = j % 2
                R.wait("tensor", gW[slot], wo_ld[j])
                if j == 0:
                    R.wait("tensor", gAM, am_v)
                flush_mom(keep=2)
                R.wait("tensor", tV, pwo_free[0])
                R.wait("tensor", tV, pwo_free[1])
                for a in range(32):
                    for tb in range(2):
                        mm(pwo[tb][:, :], wo_st[slot][:, 128 * a:128 * (a + 1)],
                           ao_mov[:, 1024 * a + 512 * tb:1024 * a + 512 * (tb + 1)],
                           a == 0, a == 31, inc=(a == 31 and tb == 1))
                wo_all = tP.v
                for tb in range(2):
                    wo_done = wo_all
                    R.wait("vector", tP, wo_done)
                    R.wait("vector", sRL[slot], xr_ld[j])
                    if j >= 2:
                        R.wait("vector", rr_store[slot][0], rr_store[slot][1])
                    R.on("vector", _i(lambda b, tb=tb, slot=slot: b.tensor_add(
                        res_row[slot][:, 512 * tb:512 * (tb + 1)], pwo[tb][:, :],
                        xres[slot][:, 512 * tb:512 * (tb + 1)]), tV))
                    pwo_free[tb] = tV.v
                    ra = tV.v
                    si = 2 * tb + j % 2
                    R.wait("scalar", tV, ra)
                    R.wait("scalar", tP, resbf_free[si])
                    R.on("scalar", lambda b, si=si, tb=tb, slot=slot: b.copy(
                        resbf[si][:, :], res_row[slot][:, 512 * tb:512 * (tb + 1)]))
                    R.on("scalar", _i(lambda b, si=si, tb=tb, slot=slot: b.activation(
                        sqbf[si][:, :], res_row[slot][:, 512 * tb:512 * (tb + 1)], AF.Square), tS))
                    sqd = tS.v
                    pending_mom.append((j, tb, sqd))
                    res_add[(j, tb)] = ra
                wo_used[slot] = tP.v
                xr_used[slot] = tV.v
                R.wait("sync", tV, res_add[(j, 1)])
                ssem = sST if j % 2 == 0 else sSTb
                R.on("sync", _i(lambda b, j=j, slot=slot: b.dma_start(
                    out=res_hbm[128 * j:128 * (j + 1), :], in_=res_row[slot][:, :]), ssem, 16))
                rr_store[slot] = (ssem, ssem.v)
                if j + 2 < 32:
                    load_wo(j + 2)
                    load_xres(j + 2)
            res_store_all = (sST.v, sSTb.v)
            flush_mom()
            mom_done = tP.v
            R.wait("vector", tP, mom_done)
            for tb in range(2):
                R.on("vector", lambda b, tb=tb: b.tensor_scalar_mul(mean_a[0:1, 512 * tb:512 * (tb + 1)], pms[tb][0:1, :], 1.0 / 4096.0))
                R.on("vector", _i(lambda b, tb=tb: b.tensor_scalar_mul(msq_a[0:1, 512 * tb:512 * (tb + 1)], pmq[tb][0:1, :], 1.0 / 4096.0), tV))
            R.wait("scalar", tV, tV.v)
            R.on("scalar", _i(lambda b: b.activation(vsd_a, mean_a, AF.Square), tS))
            R.wait("vector", tS, tS.v)
            R.on("vector", _i(lambda b: b.tensor_sub(msq_a, msq_a, vsd_a), tV))
            R.wait("scalar", tV, tV.v)
            R.on("scalar", _i(lambda b: b.activation(vsd_a, msq_a, AF.Sqrt, bias=eps_sb[0:1, 0:1]), tS))
            R.wait("vector", tS, tS.v)
            R.on("vector", _i(lambda b: b.reciprocal(rstd_a, vsd_a), tV))
            stats1_done = tV.v
            with nc.Block() as blk:
                R.flush(blk)

        esXF = contextlib.ExitStack()
        xf_bf = esXF.enter_context(nc.sbuf_tensor("xf_bf", [128, 8 * 1024], BF16))
        yl1_bf = esXF.enter_context(nc.sbuf_tensor("yl1_bf", [2, 1024], BF16))
        yl3_bf = esXF.enter_context(nc.sbuf_tensor("yl3_bf", [2, 1024], BF16))
        with contextlib.ExitStack() as esL:
            sbL = lambda n, sh, dt: esL.enter_context(nc.sbuf_tensor(n, sh, dt))
            xf_acc = sbL("xf_acc", [128, 8 * 1024], F32)
            meanB = [sbL(f"meanB{i}", [128, 512], F32) for i in range(2)]
            rstdB = [sbL(f"rstdB{i}", [128, 512], F32) for i in range(2)]
            rrow = [sbL(f"rrow{i}", [128, 1024], F32) for i in range(2)]
            tmp32 = sbL("tmp32", [128, 512], F32)
            y32 = [sbL(f"y32_{i}", [128, 512], F32) for i in range(2)]
            ybf = [sbL(f"ybf{i}", [128, 512], BF16) for i in range(2)]
            pbc = [esL.enter_context(nc.psum_tensor(f"pbc{i}", [128, 512], F32)) for i in range(2)]
            pyl = [esL.enter_context(nc.psum_tensor(f"pyl{i}", [2, 512], F32)) for i in range(4)]

            R.wait("tensor", tV, stats1_done)
            bc_done = 0
            for tb in range(2):
                if bc_done:
                    R.wait("tensor", tV, bc_done)
                mm(pbc[0][:, :], onesr_f[:, :], mean_a[0:1, 512 * tb:512 * (tb + 1)], True, True, inc=False)
                mm(pbc[1][:, :], onesr_f[:, :], rstd_a[0:1, 512 * tb:512 * (tb + 1)], True, True, inc=True)
                R.wait("vector", tP, tP.v)
                R.on("vector", lambda b, tb=tb: b.tensor_copy(meanB[tb][:, :], pbc[0][:, :]))
                R.on("vector", _i(lambda b, tb=tb: b.tensor_copy(rstdB[tb][:, :], pbc[1][:, :]), tV))
                bc_done = tV.v

            rl_used = [0, 0]
            rl_ld = {}

            def load_res(j, src, first_wait=0):
                slot = j % 2
                R.wait("sync", tV, rl_used[slot])
                if first_wait:
                    R.wait("sync", sST, first_wait[0])
                    R.wait("sync", sSTb, first_wait[1])
                R.on("sync", _i(lambda b, j=j, slot=slot, src=src: b.dma_start(
                    out=rrow[slot][:, :], in_=src[128 * j:128 * (j + 1), :]), sRL[slot], 16))
                rl_ld[j] = sRL[slot].v

            load_res(0, res_hbm, first_wait=res_store_all)
            load_res(1, res_hbm)
            ybf_used = [0, 0]
            y32_used = [0, 0]
            for j in range(32):
                slot = j % 2
                for tb in range(2):
                    pi = (2 * j + tb) % 2
                    R.wait("vector", sRL[slot], rl_ld[j])
                    if y32_used[pi]:
                        R.wait("vector", tS, y32_used[pi])
                    R.on("vector", lambda b, slot=slot, tb=tb: b.tensor_sub(
                        tmp32[:, :], rrow[slot][:, 512 * tb:512 * (tb + 1)], meanB[tb][:, :]))
                    R.on("vector", lambda b, pi=pi, tb=tb: b.tensor_mul(y32[pi][:, :], tmp32[:, :], rstdB[tb][:, :]))
                    dxf = xf_acc[:, (j % 8) * 1024 + 512 * tb:(j % 8) * 1024 + 512 * (tb + 1)]
                    if j < 8:
                        R.on("vector", _i(lambda b, d=dxf, pi=pi: b.tensor_copy(d, y32[pi][:, :]), tV))
                    else:
                        R.on("vector", _i(lambda b, d=dxf, pi=pi: b.tensor_add(d, d, y32[pi][:, :]), tV))
                    yv = tV.v
                    R.wait("scalar", tV, yv)
                    R.wait("scalar", tP, ybf_used[pi])
                    R.on("scalar", _i(lambda b, pi=pi: b.copy(ybf[pi][:, :], y32[pi][:, :]), tS))
                    y32_used[pi] = tS.v
                    R.wait("tensor", tS, tS.v)
                    mm(pyl[tb][:, :], a1_sb[:, 2 * j:2 * j + 2], ybf[pi][:, :], j == 0, j == 31, inc=False)
                    mm(pyl[2 + tb][:, :], a3_sb[:, 2 * j:2 * j + 2], ybf[pi][:, :], j == 0, j == 31, inc=True)
                    ybf_used[pi] = tP.v
                rl_used[slot] = tV.v
                if j + 2 < 32:
                    load_res(j + 2, res_hbm)
            xf_done = tV.v
            yl_done = tP.v
            R.wait("scalar", tP, yl_done)
            for tb in range(2):
                R.on("scalar", lambda b, tb=tb: b.mul(yl1_bf[0:2, 512 * tb:512 * (tb + 1)], pyl[tb][0:2, :], 0.5))
                R.on("scalar", lambda b, tb=tb: b.mul(yl3_bf[0:2, 512 * tb:512 * (tb + 1)], pyl[2 + tb][0:2, :], 0.5))
            R.wait("scalar", tV, xf_done)
            R.on("scalar", _i(lambda b: b.copy(xf_bf[:, :], xf_acc[:, :]), tS))
            if debug:
                R.wait("sync", tS, tS.v)
                R.on("sync", _i(lambda b: b.dma_start(out=dbgst[:, :], in_=stats[:, :]), sCS, 16))
                R.wait("gpsimd", tS, tS.v)
                R.on("gpsimd", _i(lambda b: b.dma_start(out=dbgxf[:, :], in_=xf_bf[:, :]), dG, 16))
                R.on("gpsimd", _i(lambda b: b.dma_start(out=dbgyl[0:2, 0:1024], in_=yl1_bf[:, :]), dG, 16))
                R.on("gpsimd", _i(lambda b: b.dma_start(out=dbgyl[0:2, 1024:2048], in_=yl3_bf[:, :]), dG, 16))
            with nc.Block() as blk:
                R.flush(blk)

        gl_bf = sb("gl_bf", [2, 1024], BF16, side="right")
        esGF = contextlib.ExitStack()
        gf_acc = esGF.enter_context(nc.sbuf_tensor("gf_acc", [128, 22 * 1024], F32, side="right"))
        with contextlib.ExitStack() as esF:
            sbF = lambda n, sh, dt: esF.enter_context(nc.sbuf_tensor(n, sh, dt))
            b1_sb = sbF("b1_sb", [2, HFF], BF16)
            b3_sb = sbF("b3_sb", [2, HFF], BF16)
            R.wait("gpsimd", dG, dG.v)
            R.on("gpsimd", _i(lambda b: b.dma_start(out=b1_sb[:, :], in_=b1T[:, :]), dG, 16))
            R.on("gpsimd", _i(lambda b: b.dma_start(out=b3_sb[:, :], in_=b3T[:, :]), dG, 16))
            bload = dG.v
            R.on("vector", _i(lambda b: b.memset(gf_acc[:, :], 0.0), tV))
            gf_zero = tV.v
            w1_st = [sbF(f"w1st{i}", [128, 8 * 256], BF16) for i in range(2)]
            w3_st = [sbF(f"w3st{i}", [128, 8 * 256], BF16) for i in range(2)]
            silu32 = [sbF(f"silu{i}", [128, 512], F32) for i in range(2)]
            g32 = [sbF(f"g32_{i}", [128, 512], F32) for i in range(2)]
            gbf = [sbF(f"gbf{i}", [128, 512], BF16) for i in range(3)]
            tmpal = sbF("tmpal", [128, 512], F32)
            ph = [esF.enter_context(nc.psum_tensor(f"ph{i}", [128, 512], F32)) for i in range(4)]
            pgl = [esF.enter_context(nc.psum_tensor(f"pgl{i}", [2, 512], F32)) for i in range(2)]

            w_used = [0, 0]
            w_ld = {}

            def load_w13(hs):
                slot = hs % 2
                R.wait("gpsimd", tP, w_used[slot])
                for w, st in ((w1qT, w1_st), (w3qT, w3_st)):
                    o = st[slot][:, :].rearrange("p (a h) -> p a h", h=256)
                    i = w[:, 256 * hs:256 * (hs + 1)].rearrange("(a p) h -> p a h", p=128)
                    R.on("gpsimd", _i(lambda b, o=o, i=i: b.dma_start(out=o, in_=i), gW[slot], 16))
                w_ld[hs] = gW[slot].v

            load_w13(0)
            load_w13(1)
            pending_gl = []

            def flush_gl(keep=0):
                while len(pending_gl) > keep:
                    h_, tb_, gi_, gv_ = pending_gl.pop(0)
                    R.wait("tensor", tS, gv_)
                    mm(pgl[tb_][:, :], a2_sb[:, 2 * h_:2 * h_ + 2], gbf[gi_][:, :],
                       h_ == 0, h_ == NHT - 1, inc=True)
                    gbf_used[gi_] = tP.v
            ph_free = [0, 0, 0, 0]
            ph_sread = [0, 0]
            ph_vread = [0, 0]
            silu_used = [0, 0]
            g32_used = [0, 0]
            gbf_used = [0, 0, 0]
            it = 0
            for hs in range(43):
                slot = hs % 2
                for ht in range(2):
                    h = 2 * hs + ht
                    for tb in range(2):
                        pi = it % 2
                        it3 = it
                        it += 1
                        R.wait("tensor", gW[slot], w_ld[hs])
                        if h == 0 and tb == 0:
                            R.wait("tensor", dG, bload)
                        flush_gl(keep=2)
                        R.wait("tensor", tS, ph_sread[pi])
                        R.wait("tensor", tV, ph_vread[pi])
                        for a in range(8):
                            mm(ph[2 * pi][:, :], w1_st[slot][:, a * 256 + 128 * ht:a * 256 + 128 * (ht + 1)],
                               xf_bf[:, 1024 * a + 512 * tb:1024 * a + 512 * (tb + 1)], a == 0, False)
                        mm(ph[2 * pi][:, :], b1_sb[0:2, 128 * h:128 * (h + 1)],
                           yl1_bf[0:2, 512 * tb:512 * (tb + 1)], False, True, inc=False)
                        for a in range(8):
                            mm(ph[2 * pi + 1][:, :], w3_st[slot][:, a * 256 + 128 * ht:a * 256 + 128 * (ht + 1)],
                               xf_bf[:, 1024 * a + 512 * tb:1024 * a + 512 * (tb + 1)], a == 0, False)
                        mm(ph[2 * pi + 1][:, :], b3_sb[0:2, 128 * h:128 * (h + 1)],
                           yl3_bf[0:2, 512 * tb:512 * (tb + 1)], False, True, inc=True)
                        hmm = tP.v
                        R.wait("scalar", tP, hmm)
                        if silu_used[pi]:
                            R.wait("scalar", tV, silu_used[pi])
                        R.on("scalar", _i(lambda b, pi=pi: b.activation(silu32[pi][:, :], ph[2 * pi][:, :], AF.Silu), tS))
                        sl = tS.v
                        ph_sread[pi] = sl
                        R.wait("vector", tS, sl)
                        if g32_used[pi]:
                            R.wait("vector", tS, g32_used[pi])
                        R.on("vector", _i(lambda b, pi=pi: b.tensor_mul(g32[pi][:, :], silu32[pi][:, :], ph[2 * pi + 1][:, :]), tV))
                        gm = tV.v
                        ph_vread[pi] = gm
                        silu_used[pi] = gm
                        R.wait("scalar", tV, gm)
                        gi = it3 % 3
                        R.wait("scalar", tP, gbf_used[gi])
                        segs = fold_segments(h)
                        need_tmp = [(so, L, dr) for (so, L, dr) in segs if so % 128 != dr % 128]
                        R.on("scalar", _i(lambda b, pi=pi, gi=gi: b.copy(gbf[gi][:, :], g32[pi][:, :]), tS))
                        gbf_done = tS.v
                        g32_used[pi] = max(gbf_done, g32_used[pi])
                        pending_gl.append((h, tb, gi, gbf_done))
                        if h == 0 and tb == 0:
                            R.wait("vector", tV, gf_zero)
                        for so, L, dr in need_tmp:
                            doff = dr % 128
                            R.on("vector", lambda b, so=so, L=L, pi=pi, doff=doff: b.tensor_copy(
                                tmpal[doff:doff + L, :], g32[pi][so:so + L, :]))
                        for n, (so, L, dr) in enumerate(segs):
                            dt_ = dr // 128
                            doff = dr % 128
                            dsl = gf_acc[doff:doff + L, 1024 * dt_ + 512 * tb:1024 * dt_ + 512 * (tb + 1)]
                            if so % 128 == doff:
                                fn = lambda b, d=dsl, pi=pi, so=so, L=L: b.tensor_add(d, d, g32[pi][so:so + L, :])
                            else:
                                fn = lambda b, d=dsl, L=L, doff=doff: b.tensor_add(d, d, tmpal[doff:doff + L, :])
                            R.on("vector", _i(fn, tV) if n == len(segs) - 1 else fn)
                w_used[slot] = tP.v
                if hs + 2 < 43:
                    load_w13(hs + 2)
            flush_gl()
            gl_done = tP.v
            fold_done = tV.v
            R.wait("scalar", tP, gl_done)
            for tb in range(2):
                fn = lambda b, tb=tb: b.mul(gl_bf[0:2, 512 * tb:512 * (tb + 1)], pgl[tb][0:2, :], 0.5)
                R.on("scalar", _i(fn, tS) if tb == 1 else fn)
            with nc.Block() as blk:
                R.flush(blk)

        esXF.close()
        gf_bf = sb("gf_bf", [128, 22 * 1024], BF16)
        with contextlib.ExitStack() as esW2:
            sbW2 = lambda n, sh, dt: esW2.enter_context(nc.sbuf_tensor(n, sh, dt))
            b2_sb = sbW2("b2_sb", [2, D], BF16)
            R.wait("gpsimd", dG, dG.v)
            R.on("gpsimd", _i(lambda b: b.dma_start(out=b2_sb[:, :], in_=b2T[:, :]), dG, 16))
            b2load = dG.v
            R.wait("scalar", tV, fold_done)
            R.on("scalar", _i(lambda b: b.copy(gf_bf[:, :], gf_acc[:, :]), tS))
            gfb_done = tS.v
            if debug:
                R.wait("gpsimd", tS, tS.v)
                R.on("gpsimd", _i(lambda b: b.dma_start(out=dbggf[:, :], in_=gf_bf[:, :]), dG, 16))
            with nc.Block() as blk:
                R.flush(blk)
            esGF.close()
            w2_st = [sbW2(f"w2st{i}", [128, 22 * 128], BF16) for i in range(2)]
            meanB = [sbW2(f"meanB2_{i}", [128, 512], F32) for i in range(2)]
            rstdB = [sbW2(f"rstdB2_{i}", [128, 512], F32) for i in range(2)]
            rrow = [sbW2(f"rrow2_{i}", [128, 1024], F32) for i in range(2)]
            r2row = [sbW2(f"r2row{i}", [128, 1024], F32) for i in range(2)]
            tmp32 = sbW2("tmp32b", [128, 512], F32)
            y32 = sbW2("y32b", [128, 512], F32)
            r2bf = [sbW2(f"r2bf{i}", [128, 512], BF16) for i in range(4)]
            sq2bf = [sbW2(f"sq2bf{i}", [128, 512], BF16) for i in range(4)]
            pw2 = [esW2.enter_context(nc.psum_tensor(f"pw2_{i}", [128, 512], F32)) for i in range(2)]
            pm2s = [esW2.enter_context(nc.psum_tensor(f"pm2s{i}", [1, 512], F32)) for i in range(2)]
            pm2q = [esW2.enter_context(nc.psum_tensor(f"pm2q{i}", [1, 512], F32)) for i in range(2)]
            pbc2 = [esW2.enter_context(nc.psum_tensor(f"pbc2_{i}", [128, 512], F32)) for i in range(2)]

            bc_done = 0
            for tb in range(2):
                if bc_done:
                    R.wait("tensor", tV, bc_done)
                mm(pbc2[0][:, :], onesr_f[:, :], mean_a[0:1, 512 * tb:512 * (tb + 1)], True, True, inc=False)
                mm(pbc2[1][:, :], onesr_f[:, :], rstd_a[0:1, 512 * tb:512 * (tb + 1)], True, True, inc=True)
                R.wait("vector", tP, tP.v)
                R.on("vector", lambda b, tb=tb: b.tensor_copy(meanB[tb][:, :], pbc2[0][:, :]))
                R.on("vector", _i(lambda b, tb=tb: b.tensor_copy(rstdB[tb][:, :], pbc2[1][:, :]), tV))
                bc_done = tV.v

            w2_used = [0, 0]
            w2_ld = {}

            def load_w2(j):
                slot = j % 2
                R.wait("gpsimd", tP, w2_used[slot])
                o1 = w2_st[slot][:, 0:21 * 128].rearrange("p (a j) -> p a j", j=128)
                i1 = w2qT[0:2688, 128 * j:128 * (j + 1)].rearrange("(a p) j -> p a j", p=128)
                R.on("gpsimd", _i(lambda b, o=o1, i=i1: b.dma_start(out=o, in_=i), gW[slot], 16))
                o2 = w2_st[slot][0:64, 21 * 128:22 * 128]
                i2 = w2qT[2688:2752, 128 * j:128 * (j + 1)]
                R.on("gpsimd", _i(lambda b, o=o2, i=i2: b.dma_start(out=o, in_=i), gW[slot], 16))
                w2_ld[j] = gW[slot].v

            rl_used = [0, 0]
            rl_ld = {}

            def load_res2(j, src, first_wait=0):
                slot = j % 2
                R.wait("sync", tV, rl_used[slot])
                if first_wait:
                    R.wait("sync", sST, first_wait)
                R.on("sync", _i(lambda b, j=j, slot=slot, src=src: b.dma_start(
                    out=rrow[slot][:, :], in_=src[128 * j:128 * (j + 1), :]), sRL[slot], 16))
                rl_ld[j] = sRL[slot].v

            load_w2(0)
            load_w2(1)
            load_res2(0, res_hbm)
            load_res2(1, res_hbm)
            pw2_free = [0, 0]
            r2bf_free = [0, 0, 0, 0]
            r2_store = [(sST, 0), (sSTb, 0)]
            pending_m2 = []

            def flush_m2(keep=0):
                while len(pending_m2) > keep:
                    j_, tb_, sqd_ = pending_m2.pop(0)
                    si = 2 * tb_ + j_ % 2
                    R.wait("tensor", tS, sqd_)
                    mm(pm2s[tb_][:, :], ones_c[:, :], r2bf[si][:, :], j_ == 0, j_ == 31, inc=False)
                    mm(pm2q[tb_][:, :], ones_c[:, :], sq2bf[si][:, :], j_ == 0, j_ == 31, inc=True)
                    r2bf_free[si] = tP.v

            for j in range(32):
                slot = j % 2
                r2a = 0
                R.wait("tensor", gW[slot], w2_ld[j])
                if j == 0:
                    R.wait("tensor", tS, gfb_done)
                    R.wait("tensor", dG, b2load)
                flush_m2(keep=2)
                R.wait("tensor", tV, pw2_free[0])
                R.wait("tensor", tV, pw2_free[1])
                for a in range(22):
                    pd = 128 if a < 21 else 64
                    for tb in range(2):
                        mm(pw2[tb][:, :], w2_st[slot][0:pd, 128 * a:128 * (a + 1)],
                           gf_bf[0:pd, 1024 * a + 512 * tb:1024 * a + 512 * (tb + 1)], a == 0, False)
                for tb in range(2):
                    mm(pw2[tb][:, :], b2_sb[0:2, 128 * j:128 * (j + 1)],
                       gl_bf[0:2, 512 * tb:512 * (tb + 1)], False, True, inc=(tb == 1))
                w2all = tP.v
                for tb in range(2):
                    w2done = w2all
                    R.wait("vector", tP, w2done)
                    R.wait("vector", sRL[slot], rl_ld[j])
                    if j >= 2:
                        R.wait("vector", r2_store[slot][0], r2_store[slot][1])
                    R.on("vector", lambda b, slot=slot, tb=tb: b.tensor_sub(
                        tmp32[:, :], rrow[slot][:, 512 * tb:512 * (tb + 1)], meanB[tb][:, :]))
                    R.on("vector", lambda b, tb=tb: b.tensor_mul(y32[:, :], tmp32[:, :], rstdB[tb][:, :]))
                    R.on("vector", _i(lambda b, slot=slot, tb=tb: b.tensor_add(
                        r2row[slot][:, 512 * tb:512 * (tb + 1)], y32[:, :], pw2[tb][:, :]), tV))
                    pw2_free[tb] = tV.v
                    r2a = tV.v
                    si = 2 * tb + j % 2
                    R.wait("scalar", tV, r2a)
                    R.wait("scalar", tP, r2bf_free[si])
                    R.on("scalar", lambda b, si=si, tb=tb, slot=slot: b.copy(
                        r2bf[si][:, :], r2row[slot][:, 512 * tb:512 * (tb + 1)]))
                    R.on("scalar", _i(lambda b, si=si, tb=tb, slot=slot: b.activation(
                        sq2bf[si][:, :], r2row[slot][:, 512 * tb:512 * (tb + 1)], AF.Square), tS))
                    pending_m2.append((j, tb, tS.v))
                w2_used[slot] = tP.v
                rl_used[slot] = tV.v
                R.wait("sync", tV, r2a)
                ssem = sST if j % 2 == 0 else sSTb
                R.on("sync", _i(lambda b, j=j, slot=slot: b.dma_start(
                    out=res2_hbm[128 * j:128 * (j + 1), :], in_=r2row[slot][:, :]), ssem, 16))
                r2_store[slot] = (ssem, ssem.v)
                if j + 2 < 32:
                    load_w2(j + 2)
                    load_res2(j + 2, res_hbm)
            res2_store_all = (sST.v, sSTb.v)
            flush_m2()
            mom_done = tP.v
            R.wait("vector", tP, mom_done)
            for tb in range(2):
                R.on("vector", lambda b, tb=tb: b.tensor_scalar_mul(mean_b[0:1, 512 * tb:512 * (tb + 1)], pm2s[tb][0:1, :], 1.0 / 4096.0))
                R.on("vector", _i(lambda b, tb=tb: b.tensor_scalar_mul(msq_a[0:1, 512 * tb:512 * (tb + 1)], pm2q[tb][0:1, :], 1.0 / 4096.0), tV))
            R.wait("scalar", tV, tV.v)
            R.on("scalar", _i(lambda b: b.activation(vsd_a, mean_b, AF.Square), tS))
            R.wait("vector", tS, tS.v)
            R.on("vector", _i(lambda b: b.tensor_sub(msq_a, msq_a, vsd_a), tV))
            R.wait("scalar", tV, tV.v)
            R.on("scalar", _i(lambda b: b.activation(vsd_a, msq_a, AF.Sqrt, bias=eps_sb[0:1, 0:1]), tS))
            R.wait("vector", tS, tS.v)
            R.on("vector", _i(lambda b: b.reciprocal(rstd_b, vsd_a), tV))
            stats2_done = tV.v
            with nc.Block() as blk:
                R.flush(blk)

        with contextlib.ExitStack() as esO:
            sbO = lambda n, sh, dt: esO.enter_context(nc.sbuf_tensor(n, sh, dt))
            meanB = [sbO(f"meanBo{i}", [128, 512], F32) for i in range(2)]
            rstdB = [sbO(f"rstdBo{i}", [128, 512], F32) for i in range(2)]
            rrow = [sbO(f"rrowo{i}", [128, 1024], F32) for i in range(2)]
            orow = [sbO(f"orow{i}", [128, 1024], F32) for i in range(2)]
            tmp32 = sbO("tmp32o", [128, 512], F32)
            pbcO = [esO.enter_context(nc.psum_tensor(f"pbcO{i}", [128, 512], F32)) for i in range(2)]

            R.wait("tensor", tV, stats2_done)
            bc_done = 0
            for tb in range(2):
                if bc_done:
                    R.wait("tensor", tV, bc_done)
                mm(pbcO[0][:, :], onesr_f[:, :], mean_b[0:1, 512 * tb:512 * (tb + 1)], True, True, inc=False)
                mm(pbcO[1][:, :], onesr_f[:, :], rstd_b[0:1, 512 * tb:512 * (tb + 1)], True, True, inc=True)
                R.wait("vector", tP, tP.v)
                R.on("vector", lambda b, tb=tb: b.tensor_copy(meanB[tb][:, :], pbcO[0][:, :]))
                R.on("vector", _i(lambda b, tb=tb: b.tensor_copy(rstdB[tb][:, :], pbcO[1][:, :]), tV))
                bc_done = tV.v

            rl_used = [0, 0]
            rl_ld = {}

            def load_r2o(j, first_wait=0):
                slot = j % 2
                R.wait("sync", tV, rl_used[slot])
                if first_wait:
                    R.wait("sync", sST, first_wait[0])
                    R.wait("sync", sSTb, first_wait[1])
                R.on("sync", _i(lambda b, j=j, slot=slot: b.dma_start(
                    out=rrow[slot][:, :], in_=res2_hbm[128 * j:128 * (j + 1), :]), sRL[slot], 16))
                rl_ld[j] = sRL[slot].v

            load_r2o(0, first_wait=res2_store_all)
            load_r2o(1)
            o_store = [(sOut, 0), (sOutb, 0)]
            for j in range(32):
                slot = j % 2
                R.wait("vector", sRL[slot], rl_ld[j])
                if j >= 2:
                    R.wait("vector", o_store[slot][0], o_store[slot][1])
                for tb in range(2):
                    R.on("vector", lambda b, slot=slot, tb=tb: b.tensor_sub(
                        tmp32[:, :], rrow[slot][:, 512 * tb:512 * (tb + 1)], meanB[tb][:, :]))
                    fn = lambda b, slot=slot, tb=tb: b.tensor_mul(
                        orow[slot][:, 512 * tb:512 * (tb + 1)], tmp32[:, :], rstdB[tb][:, :])
                    R.on("vector", _i(fn, tV) if tb == 1 else fn)
                ov = tV.v
                rl_used[slot] = ov
                R.wait("sync", tV, ov)
                osem = sOut if j % 2 == 0 else sOutb
                R.on("sync", _i(lambda b, j=j, slot=slot: b.dma_start(
                    out=outT[128 * j:128 * (j + 1), :], in_=orow[slot][:, :]), osem, 16))
                o_store[slot] = (osem, osem.v)
                if j + 2 < 32:
                    load_r2o(j + 2)
            R.wait("gpsimd", sOut, sOut.v)
            R.wait("gpsimd", sOutb, sOutb.v)
            R.wait("gpsimd", sST, sST.v)
            R.wait("gpsimd", sSTb, sSTb.v)
            R.wait("gpsimd", dG, dG.v)
            with nc.Block() as blk:
                R.flush(blk)
    return nc


_NC_CACHE = {}


def _get_nc():
    if "nc" not in _NC_CACHE:
        _NC_CACHE["nc"] = build_nc()
    return _NC_CACHE["nc"]


def _host_constants():
    inv = 1.0 / (10000.0 ** (np.arange(0, 128, 2, dtype=np.float64) / 128.0))
    pos = np.arange(S, dtype=np.float64)
    fr = pos[:, None] * inv[None, :]
    emb = np.concatenate([fr, fr], axis=1)
    cosT_h = np.ascontiguousarray(np.cos(emb).T.astype(np.float32))
    sinT = np.sin(emb).T.astype(np.float32)
    sinsT_h = sinT.copy()
    sinsT_h[0:64] *= -1.0
    sinsT_h = np.ascontiguousarray(sinsT_h)
    m0 = np.where(np.arange(128)[:, None] <= np.arange(128)[None, :], 0.0, -1e9).astype(np.float32)
    ident = np.eye(128, dtype=np.float32)
    return cosT_h, sinsT_h, m0, ident


def _swz(a):
    n = a.shape[0] // 128
    r = a.shape[1]
    return np.ascontiguousarray(a.reshape(n, 128, r).transpose(1, 0, 2).reshape(128, n * r))


def kernel(L0_attn_hidden_states, L0_attn_Wq, L0_attn_Wk, L0_attn_Wv, L0_attn_Wo,
           L0_ffn_w1q, L0_ffn_a1, L0_ffn_b1, L0_ffn_w3q, L0_ffn_a3, L0_ffn_b3,
           L0_ffn_w2q, L0_ffn_a2, L0_ffn_b2):
    X = np.asarray(L0_attn_hidden_states, dtype=np.float32)
    Wq = np.asarray(L0_attn_Wq, np.float32)
    Wk = np.asarray(L0_attn_Wk, np.float32)
    Wv = np.asarray(L0_attn_Wv, np.float32)
    Wo = np.asarray(L0_attn_Wo, np.float32)
    w1q = np.asarray(L0_ffn_w1q, np.float32)
    w3q = np.asarray(L0_ffn_w3q, np.float32)
    w2q = np.asarray(L0_ffn_w2q, np.float32)
    a1 = np.asarray(L0_ffn_a1, np.float32)
    a3 = np.asarray(L0_ffn_a3, np.float32)
    a2 = np.asarray(L0_ffn_a2, np.float32)
    b1 = np.asarray(L0_ffn_b1, np.float32)
    b3 = np.asarray(L0_ffn_b3, np.float32)
    b2 = np.asarray(L0_ffn_b2, np.float32)

    cosT_h, sinsT_h, m0, ident = _host_constants()
    XT = [np.ascontiguousarray(X[b].T) for b in range(B)]
    woT_h = np.ascontiguousarray(Wo.T)
    w1qT_h = np.ascontiguousarray(w1q.T)
    w3qT_h = np.ascontiguousarray(w3q.T)
    w2qT_h = np.ascontiguousarray(w2q.T)
    a1R = _swz(np.ascontiguousarray(a1.T))
    a3R = _swz(np.ascontiguousarray(a3.T))
    a2R = _swz(np.ascontiguousarray(a2.T))
    b1T_h = np.ascontiguousarray(b1.T)
    b3T_h = np.ascontiguousarray(b3.T)
    b2T_h = np.ascontiguousarray(b2.T)

    in_maps = []
    for c in range(8):
        b = c // 4
        r = c % 4
        a_ = np.arange(32)[None, :]
        hl = a_ % 8
        row = np.where(hl < 4, 512 * (a_ // 8) + 128 * hl,
                       np.where(hl < 7, 384 * (a_ // 8) + 128 * (hl - 4),
                                128 * (a_ // 8)))
        idx = (row + np.arange(128)[:, None]) * S + 1024 * r
        in_maps.append({
            "xt": XT[b],
            "xts": np.ascontiguousarray(XT[b][:, 1024 * r:1024 * (r + 1)]),
            "wqT": np.ascontiguousarray(Wq[1024 * r:1024 * (r + 1), :].T),
            "wkT": np.ascontiguousarray(Wk[256 * r:256 * (r + 1), :].T),
            "wvT": np.ascontiguousarray(Wv[256 * r:256 * (r + 1), :].T),
            "woT": woT_h,
            "w1qT": w1qT_h, "w3qT": w3qT_h, "w2qT": w2qT_h,
            "a1R": a1R, "a3R": a3R, "a2R": a2R,
            "b1T": b1T_h, "b3T": b3T_h, "b2T": b2T_h,
            "cosT": cosT_h, "sinsT": sinsT_h,
            "m0": m0, "ident": ident,
            "idxg": idx.astype(np.int32),
        })
    nc = _get_nc()
    res = run_bass_kernel_spmd(nc, in_maps, core_ids=list(range(8)))
    out = np.empty((B, S, D), dtype=np.float32)
    for c in range(8):
        b = c // 4
        r = c % 4
        out[b, 1024 * r:1024 * (r + 1), :] = res.results[c]["outT"].T
    return out
